# version 58
# baseline (speedup 1.0000x reference)
import numpy as np
import concourse.bass as bass
import concourse.mybir as mybir
from concourse.bass_utils import run_bass_kernel_spmd

F32 = mybir.dt.float32
I32 = mybir.dt.int32
AF = mybir.ActivationFunctionType
ALU = mybir.AluOpType
AX = mybir.AxisListType

D = 1024
H = 16
T = 128
NT = 16
NP = 20
MHAT = 30.0
KILL = -1e9
VW = H * 65

_WO_PERM = np.concatenate(
    [np.concatenate([np.arange(64 * c, 64 * c + 64),
                     np.arange(512 + 64 * c, 512 + 64 * c + 64)])
     for c in range(8)])

ENGINES = ("pe", "act", "dve", "gp", "sp")
NROT = 4


class Buf:
    __slots__ = ("name", "writers", "readers")

    def __init__(self, name):
        self.name = name
        self.writers = {}
        self.readers = {}


class Op:
    __slots__ = ("idx", "eng", "build", "deps", "need_inc", "is_dma",
                 "dma_k", "waits", "sem_val", "sem_key", "hard", "stream")

    def __init__(self, idx, eng, build, is_dma):
        self.idx = idx
        self.eng = eng
        self.build = build
        self.deps = set()
        self.need_inc = False
        self.is_dma = is_dma
        self.dma_k = -1
        self.waits = []
        self.sem_val = 0
        self.sem_key = None
        self.hard = set()
        self.stream = True


class Prog:
    def __init__(self):
        self.ops = []
        self.dma_count = {}
        self.pending_barrier = None
        self._bar_seen = set()
        self._dma_idx = {}

    def _record(self, eng, build, reads, writes, is_dma):
        i = len(self.ops)
        op = Op(i, eng, build, is_dma)
        for b in reads:
            for j in b.writers.values():
                op.deps.add(j)
            b.readers[eng + str(is_dma) + str(i)] = i
        for b in writes:
            for j in b.writers.values():
                op.deps.add(j)
            for j in b.readers.values():
                op.deps.add(j)
            b.writers = {eng: i}
            b.readers = {}
        op.deps.discard(i)
        if self.pending_barrier is not None and eng not in self._bar_seen:
            self._bar_seen.add(eng)
            for j in self.pending_barrier:
                if j != i:
                    op.deps.add(j)
        if is_dma:
            k = self.dma_count.get(eng, 0)
            self.dma_count[eng] = k + 1
            op.dma_k = k
            if k >= NROT:
                prev = self._dma_idx[(eng, k - NROT)]
                self.ops[prev].need_inc = True
                op.deps.add(prev)
            self._dma_idx[(eng, k)] = i
        self.ops.append(op)
        return i

    def op(self, eng, build, reads=(), writes=(), stream=True):
        i = self._record(eng, build, reads, writes, False)
        self.ops[i].hard = getattr(self.ops[i], "hard", set())
        if not stream:
            self.ops[i].stream = False
        return i

    def hard_dep(self, i, j):
        self.ops[i].hard = getattr(self.ops[i], "hard", set()) | {j}

    def dma(self, eng, build, reads=(), writes=()):
        return self._record(eng, build, reads, writes, True)

    def barrier(self):
        last = {}
        for o in self.ops:
            if o.is_dma:
                last[("dma", o.eng, o.dma_k % NROT)] = o.idx
            else:
                last[o.eng] = o.idx
        idxs = sorted(set(last.values()))
        for j in idxs:
            self.ops[j].need_inc = True
        self.pending_barrier = idxs
        self._bar_seen = set()

    def finalize(self):
        for o in self.ops:
            if o.eng == "pe" and o.stream:
                o.deps = {j for j in o.deps if self.ops[j].eng != "pe"
                          or self.ops[j].is_dma or not self.ops[j].stream}
                o.deps |= o.hard
        for o in self.ops:
            for j in o.deps:
                self.ops[j].need_inc = True
        counts = {}
        for o in self.ops:
            if o.is_dma:
                key = ("dma", o.eng, o.dma_k % NROT)
                amt = 16
            else:
                key = o.eng
                amt = 1
            o.sem_key = key
            if o.need_inc:
                counts[key] = counts.get(key, 0) + amt
                o.sem_val = counts[key]
        marks = {e: {} for e in ENGINES}
        for o in self.ops:
            need = {}
            for j in o.deps:
                p = self.ops[j]
                k, v = p.sem_key, p.sem_val
                if need.get(k, 0) < v:
                    need[k] = v
            m = marks[o.eng]
            for k in sorted(need, key=str):
                v = need[k]
                if m.get(k, 0) < v:
                    o.waits.append((k, v))
                    m[k] = v
        return counts

    def emit(self, eng_obj, eng_name, sems):
        for o in self.ops:
            if o.eng != eng_name:
                continue
            for k, v in o.waits:
                eng_obj.wait_ge(sems[k], v)
            if o.build is None:
                continue
            inst = o.build(eng_obj)
            if o.need_inc:
                inst.then_inc(sems[o.sem_key], 16 if o.is_dma else 1)


def _rope_vectors():
    HD = 64
    inv = 1.0 / (10000.0 ** (np.arange(0, HD, 2, dtype=np.float64) / HD))
    freqs = np.arange(H, dtype=np.float64)[:, None] * inv[None, :]
    emb = np.concatenate([freqs, freqs], axis=-1)
    cos = np.cos(emb).astype(np.float32).reshape(-1)
    sin = np.sin(emb).astype(np.float32).reshape(-1)
    sgn = np.where(np.arange(D) % 2 == 0, -1.0, 1.0).astype(np.float32)
    return cos, sin * sgn


def build_nc():
    nc = bass.Bass("TRN2", target_bir_lowering=False)

    xp = nc.dram_tensor("xp", [NP * T, D], F32, kind="ExternalInput").ap()
    w_in = {}
    for nm in ("wq", "wk", "wv", "wo", "wf"):
        w_in[nm] = nc.dram_tensor(nm, [D, D], F32, kind="ExternalInput").ap()
    b_in = {}
    for nm in ("bq", "bk", "bv", "bo", "bf"):
        b_in[nm] = nc.dram_tensor(nm, [1, D], F32, kind="ExternalInput").ap()
    killb_in = nc.dram_tensor("killb_hbm", [128, NP], F32, kind="ExternalInput").ap()
    out_hbm = nc.dram_tensor("out", [NT * T, D], F32, kind="ExternalOutput").ap()
    wf_scr_hi = nc.dram_tensor("wf_scr_hi", [128, 8 * D],
                               mybir.dt.bfloat16, kind="Internal").ap()
    wf_scr_lo = nc.dram_tensor("wf_scr_lo", [128, 8 * D],
                               mybir.dt.bfloat16, kind="Internal").ap()
    ctx_scr_hi = nc.dram_tensor("ctx_scr_hi", [NT * T, D],
                                mybir.dt.bfloat16, kind="Internal").ap()
    ctx_scr_lo = nc.dram_tensor("ctx_scr_lo", [NT * T, D],
                                mybir.dt.bfloat16, kind="Internal").ap()

    cosv, sinv = _rope_vectors()
    cos_c = nc.inline_tensor(np.repeat(cosv[None, :], 128, 0), "cos_c").ap()
    sin_c = nc.inline_tensor(np.repeat(sinv[None, :], 128, 0), "sin_c").ap()
    ident_c = nc.inline_tensor(np.eye(128, dtype=np.float32), "ident_c").ap()
    ii, jj = np.indices((128, 128))
    m0_c = nc.inline_tensor((ii >= jj).astype(np.float32), "m0_c").ap()
    c192_c = nc.inline_tensor(np.full((128, 1), 192.0, np.float32),
                              "c192_c").ap()
    c8_c = nc.inline_tensor(np.full((128, 1), 8.0, np.float32), "c8_c").ap()
    ceps_c = nc.inline_tensor(np.full((128, 1), 1e-5, np.float32),
                              "ceps_c").ap()
    m4_c = nc.inline_tensor((ii <= jj).astype(np.float32), "m4_c").ap()

    P = Prog()
    guards = []

    def sb(name, shape, scope=None):
        g = nc.sbuf_tensor(name, shape, F32)
        t = g.__enter__()
        (guards if scope is None else scope).append(g)
        return t

    pb = []
    for i in range(4):
        g = nc.psum_tensor(f"pb{i}", [128, 512], F32)
        pb.append(g.__enter__())
        guards.append(g)
    pbuf = [Buf(f"pb{i}") for i in range(4)]
    g = nc.psum_tensor("psE", [128, 1024], F32)
    psE = g.__enter__(); guards.append(g)
    g = nc.psum_tensor("psO", [128, 1024], F32)
    psO = g.__enter__(); guards.append(g)
    psE_b = Buf("psE"); psO_b = Buf("psO")

    ident = sb("ident", [128, 128]);   ident_b = Buf("ident")
    m0 = sb("m0", [128, 128]);         m0_b = Buf("m0")
    m4 = sb("m4", [128, 128]);         m4_b = Buf("m4")
    killb = sb("killb_sb", [128, NP]);  killb_b = Buf("killb")
    c192 = sb("c192", [128, 1]);       c192_b = Buf("c192")
    c8 = sb("c8", [128, 1]);           c8_b = Buf("c8")
    ceps = sb("ceps", [128, 1]);       ceps_b = Buf("ceps")
    bias_fm = {n: sb("fm_" + n, [128, 8]) for n in b_in}
    bias_fm_b = {n: Buf("fm_" + n) for n in b_in}
    wf_scr_b = Buf("wf_scr")

    def dma_load(queue, dst_ap, src_ap, reads, writes):
        P.dma(queue, lambda e, d=dst_ap, s=src_ap: e.dma_start(out=d, in_=s),
              reads=list(reads), writes=list(writes))

    def mm(lhsT, rhs, o, start, stop, reads, writes, tp=None, stream=True):
        P.op("pe", lambda e, a=lhsT, b=rhs, c=o, s1=start, s2=stop, t=tp:
             e.matmul(c, a, b, start=s1, stop=s2) if t is None else
             e.matmul(c, a, b, start=s1, stop=s2, tile_position=t),
             reads=list(reads), writes=list(writes), stream=stream)

    def transp(in_ap, out_ap, idn, reads, writes):
        P.op("pe", lambda e, a=in_ap, o=out_ap, i=idn: e.transpose(o, a, i),
             reads=list(reads), writes=list(writes))

    def act(out_ap, in_ap, func, bias, scale, reads, writes, accum=None):
        P.op("act", lambda e, o=out_ap, i=in_ap, f=func, b=bias, s=scale,
             a=accum: e.activation(o, i, f, bias=b, scale=s, accum_out=a),
             reads=list(reads), writes=list(writes))

    def tt(eng, out_ap, a_ap, b_ap, opk, reads, writes):
        P.op(eng, lambda e, o=out_ap, a=a_ap, b=b_ap, k=opk:
             e.tensor_tensor(o, a, b, op=k),
             reads=list(reads), writes=list(writes))

    def ts2(eng, out_ap, in_ap, s1, s2, op1, op2, reads, writes):
        P.op(eng, lambda e, o=out_ap, i=in_ap, a=s1, b=s2, k1=op1, k2=op2:
             e.tensor_scalar(o, i, a, b, k1, k2),
             reads=list(reads), writes=list(writes))

    def ts1(eng, out_ap, in_ap, s1, op1, reads, writes):
        P.op(eng, lambda e, o=out_ap, i=in_ap, a=s1, k1=op1:
             e.tensor_single_scalar(o, i, a, k1),
             reads=list(reads), writes=list(writes))

    def vcopy(eng, out_ap, in_ap, reads, writes):
        P.op(eng, lambda e, o=out_ap, i=in_ap: e.tensor_copy(o, i),
             reads=list(reads), writes=list(writes))

    def vreduce(out_ap, in_ap, opk, reads, writes, absval=False):
        P.op("dve", lambda e, o=out_ap, i=in_ap, k=opk, av=absval:
             e.tensor_reduce(o, i, axis=AX.X, op=k, apply_absolute_value=av),
             reads=list(reads), writes=list(writes))

    def vrecip(out_ap, in_ap, reads, writes):
        P.op("dve", lambda e, o=out_ap, i=in_ap: e.reciprocal(o, i),
             reads=list(reads), writes=list(writes))

    def split_evac(psum_ap, psum_buf, hi_ap, lo_ap, wbufs):
        act(hi_ap, psum_ap, AF.Identity, 0.0, 1.0, [psum_buf], list(wbufs))
        tt("dve", lo_ap, psum_ap, hi_ap, ALU.subtract,
           [psum_buf] + list(wbufs), list(wbufs))

    def strided(t_ap, off, dims):
        return bass.AP(t_ap.tensor, t_ap.offset + off,
                       [list(t_ap.ap[0])] + [list(d) for d in dims])

    dma_load("sp", ident[:], ident_c, [], [ident_b])
    dma_load("sp", m0[:], m0_c, [], [m0_b])
    dma_load("sp", m4[:], m4_c, [], [m4_b])
    dma_load("sp", killb[:], killb_in, [], [killb_b])
    dma_load("sp", c192[:], c192_c, [], [c192_b])
    dma_load("sp", c8[:], c8_c, [], [c8_b])
    dma_load("sp", ceps[:], ceps_c, [], [ceps_b])

    def weight_to_fm(w_hbm, wraw_ap, wraw_b, fm_t, fm_b, lo_t=None,
                     evac_scale=1.0, queue="sp", n_pre=0):
        for jt in range(8):
            sl = jt % 2
            if jt >= n_pre:
                dma_load(queue, wraw_ap[sl],
                         w_hbm[jt * 128:(jt + 1) * 128, :],
                         [], [wraw_b[sl]])
            for half in range(2):
                bank = pb[half]
                for k4 in range(4):
                    kc = half * 4 + k4
                    transp(wraw_ap[sl][:, kc * 128:(kc + 1) * 128],
                           bank[:, k4 * 128:(k4 + 1) * 128], ident[:],
                           [wraw_b[sl], ident_b], [pbuf[half]])
                off = (half * 4) * D + jt * 128
                src3 = bank[:].rearrange("p (a b) -> p a b", b=128)
                if lo_t is None:
                    act(strided(fm_t[:], off, [[D, 4], [1, 128]]), src3,
                        AF.Identity, 0.0, evac_scale, [pbuf[half]], [fm_b])
                else:
                    split_evac(src3, pbuf[half],
                               strided(fm_t[:], off, [[D, 4], [1, 128]]),
                               strided(lo_t[:], off, [[D, 4], [1, 128]]),
                               [fm_b])

    def swap_ap(t_ap, ncols):
        return bass.AP(t_ap.tensor, t_ap.offset + 1,
                       [list(t_ap.ap[0]), [2, ncols // 2], [-1, 2]])

    def pair3(t_ap):
        return t_ap.rearrange("p (a b) -> p a b", b=2)

    def make_quantize(scope, W=D, use_act=False, alias=None):
        NB32 = W // 32
        if alias is None:
            qs_bm = sb("qsW_bm" + scope[0], [128, NB32], scope[1])
            qs_sc = sb("qsW_sc" + scope[0], [128, NB32], scope[1])
            qs_rs = sb("qsW_rs" + scope[0], [128, NB32], scope[1])
            qsc = [sb(f"qscW{i}" + scope[0], [128, W], scope[1])
                   for i in range(6)]
            qbf = sb16("qbfW" + scope[0], [128, W], scope[1])
            qs_bm_b = Buf("qs_bm" + scope[0])
            qs_sc_b = Buf("qs_sc" + scope[0])
            qs_rs_b = Buf("qs_rs" + scope[0])
            qsc_b = [Buf(f"qsc{i}" + scope[0]) for i in range(6)]
            qbf_b = Buf("qbf" + scope[0])
        else:
            host, hb = alias
            qsc = [host[:, i * W:(i + 1) * W] for i in range(6)]
            qbf = host[:, 6 * W:6 * W + W // 2].bitcast(mybir.dt.bfloat16)
            qs_bm = host[:, 13 * W // 2:13 * W // 2 + NB32]
            qs_sc = host[:, 13 * W // 2 + NB32:13 * W // 2 + 2 * NB32]
            qs_rs = host[:, 13 * W // 2 + 2 * NB32:13 * W // 2 + 3 * NB32]
            qs_bm_b = qs_sc_b = qs_rs_b = qbf_b = hb
            qsc_b = [hb] * 6

        def quant_steps(src_ap, src_b, dst_ap, dst_b):
            s3 = lambda a: a.rearrange("p (b k) -> p b k", k=32)
            bcast = lambda a: a.unsqueeze(2).broadcast_to([128, NB32, 32])
            Tq, Aq, G2, G4, Sq, Rq = qsc
            Tq_b, Aq_b, G2_b, G4_b, Sq_b, Rq_b = qsc_b
            st = []
            st.append(lambda: vreduce(
                qs_bm[:].rearrange("p (b o) -> p b o", o=1),
                s3(src_ap), ALU.max, [src_b], [qs_bm_b], absval=True))
            st.append(lambda: ts2(
                "dve", qs_sc[:], qs_bm[:], 1.0 / 7.5, 1e-30, ALU.mult,
                ALU.max, [qs_bm_b], [qs_sc_b]))
            st.append(lambda: vrecip(qs_rs[:], qs_sc[:],
                                     [qs_sc_b], [qs_rs_b]))
            st.append(lambda: tt(
                "dve", s3(Tq[:]), s3(src_ap), bcast(qs_rs[:]), ALU.mult,
                [src_b, qs_rs_b], [Tq_b]))
            if use_act:
                st.append(lambda: act(Aq[:], Tq[:], AF.Abs, 0.0, 1.0,
                                      [Tq_b], [Aq_b]))
                thr2, thr4 = 2.0, 4.0
            else:
                st.append(lambda: tt("dve", Aq[:], Tq[:], Tq[:], ALU.mult,
                                     [Tq_b], [Aq_b]))
                thr2, thr4 = 4.0, 16.0
            st.append(lambda: ts1("dve", G2[:], Aq[:], thr2, ALU.is_ge,
                                  [Aq_b], [G2_b]))
            st.append(lambda: ts1("dve", G4[:], Aq[:], thr4, ALU.is_ge,
                                  [Aq_b], [G4_b]))
            st.append(lambda: P.op(
                "dve", lambda e, o=Rq[:], a=G2[:], b=G4[:]:
                e.scalar_tensor_tensor(o, a, 2.0, b, ALU.mult, ALU.add),
                reads=[G2_b, G4_b], writes=[Rq_b]))
            if use_act:
                st.append(lambda: act(Rq[:], Rq[:], AF.Identity, c8[:],
                                      -2.0, [Rq_b, c8_b], [Rq_b]))
            else:
                st.append(lambda: ts2("dve", Rq[:], Rq[:], -2.0, 8.0,
                                      ALU.mult, ALU.add, [Rq_b], [Rq_b]))
            st.append(lambda: ts2(
                "dve", Sq[:].bitcast(I32), Rq[:].bitcast(I32),
                -1, 0x7F000000, ALU.mult, ALU.add, [Rq_b], [Sq_b]))
            st.append(lambda: tt("dve", Aq[:], Tq[:], Rq[:], ALU.mult,
                                 [Tq_b, Rq_b], [Aq_b]))
            if use_act:
                st.append(lambda: act(qbf[:], Aq[:], AF.Identity, c192[:],
                                      1.0, [Aq_b, c192_b], [qbf_b]))
            else:
                st.append(lambda: ts1("dve", qbf[:], Aq[:], 192.0, ALU.add,
                                      [Aq_b], [qbf_b]))
            st.append(lambda: P.op(
                "dve", lambda e, o=Tq[:], i=qbf[:], s=Sq[:]:
                e.scalar_tensor_tensor(o, i, -192.0, s, ALU.add, ALU.mult),
                reads=[qbf_b, Sq_b], writes=[Tq_b]))
            st.append(lambda: tt(
                "dve", s3(dst_ap), s3(Tq[:]), bcast(qs_sc[:]), ALU.mult,
                [Tq_b, qs_sc_b], list(dst_b)))
            return st

        def quantize(src_ap, src_b, dst_ap, dst_b):
            for f in quant_steps(src_ap, src_b, dst_ap, dst_b):
                f()
        quantize.steps = quant_steps
        quantize.scratch = (qsc, qsc_b)
        return quantize

    w_scope = []
    ws_scope = []

    def sb16(name, shape, scope):
        g = nc.sbuf_tensor(name, shape, mybir.dt.bfloat16)
        t = g.__enter__()
        scope.append(g)
        return t

    wq_hi = sb16("wq_hi", [128, 8 * D], w_scope)
    wq_lo = sb16("wq_lo", [128, 8 * D], w_scope)
    wk_hi = sb16("wk_hi", [128, 8 * D], w_scope)
    wk_lo = sb16("wk_lo", [128, 8 * D], w_scope)
    wv_hi = sb16("wv_hi", [128, 8 * D], w_scope)
    wv_lo = sb16("wv_lo", [128, 8 * D], w_scope)
    wq_fm_b = Buf("wq_fm"); wk_fm_b = Buf("wk_fm"); wv_fm_b = Buf("wv_fm")
    xa_scope = []
    x_tm = sb("x_tm", [128, 2 * D], xa_scope)
    x_hi = sb16("x_hi", [128, 4 * D], xa_scope)
    x_lo = sb16("x_lo", [128, 4 * D], xa_scope)
    cosr = x_tm[:, 0:D];       cosr_b = Buf("cosr")
    sinr = x_tm[:, D:2 * D];   sinr_b = Buf("sinr")

    def wtv(i, r0, r1, c0, c1):
        return x_hi[r0:r1, 2 * (i * D + c0):2 * (i * D + c1)].bitcast(F32)

    def btv(n, r0, r1, c0, c1):
        base = 0 if n == "bq" else D
        return x_lo[r0:r1, 2 * (base + c0):2 * (base + c1)].bitcast(F32)
    wtmp_b = [Buf(f"wtmp{i}") for i in range(2)]
    bias_tm_b = {n: Buf("tm_" + n) for n in ("bq", "bk")}
    wraw = [sb(f"wraw{i}", [128, D], ws_scope) for i in range(3)]
    wraw_b = [Buf(f"wraw{i}") for i in range(3)]
    wfp = sb("wfp", [128, 8 * D], ws_scope);  wfp_b = Buf("wfp")
    wvraw = [sb(f"wvraw{i}", [128, D], ws_scope) for i in range(2)]
    wvraw_b = [Buf(f"wvraw{i}") for i in range(2)]
    for n in ("bq", "bk"):
        dma_load("act", btv(n, 0, 1, 0, D), b_in[n], [], [bias_tm_b[n]])


    dma_load("act", cosr, cos_c, [], [cosr_b])
    dma_load("act", sinr, sin_c, [], [sinr_b])

    def split_weight(fp_t, fp_b, hi_t, lo_t, fm_b):
        for kc in range(8):
            c = fp_t[:, kc * D:(kc + 1) * D]
            act(hi_t[:, kc * D:(kc + 1) * D], c, AF.Identity, 0.0, 1.0,
                [fp_b], [fm_b])
            tt("dve", lo_t[:, kc * D:(kc + 1) * D], c,
               hi_t[:, kc * D:(kc + 1) * D], ALU.subtract,
               [fp_b, fm_b], [fm_b])

    def rope_fold(fm_t, fm_b):
        for kc in range(8):
            w_sl = fm_t[:, kc * D:(kc + 1) * D]
            tt("dve", wtv(0, 0, 128, 0, D), w_sl, cosr, ALU.mult,
               [fm_b, cosr_b], [wtmp_b[0]])
            tt("dve", pair3(wtv(1, 0, 128, 0, D)), swap_ap(w_sl, D),
               pair3(sinr), ALU.mult, [fm_b, sinr_b], [wtmp_b[1]])
            tt("dve", w_sl, wtv(0, 0, 128, 0, D), wtv(1, 0, 128, 0, D),
               ALU.add, [wtmp_b[0], wtmp_b[1]], [fm_b])

    quantA = make_quantize(("w", ws_scope), use_act=True)
    qsc_s, qsc_sb = quantA.scratch
    for sl in range(2):
        dma_load("act", qsc_s[sl][:],
                 w_in["wv"][sl * 128:(sl + 1) * 128, :], [], [qsc_sb[sl]])
        dma_load("act", wvraw[sl][:],
                 w_in["wv"][(2 + sl) * 128:(3 + sl) * 128, :],
                 [], [wvraw_b[sl]])
        dma_load("gp", qsc_s[2 + sl][:],
                 w_in["wk"][sl * 128:(sl + 1) * 128, :],
                 [], [qsc_sb[2 + sl]])

    def wv_block(src_t, src_b, vjt, vhalf, bank_i):
        bank = pb[bank_i]
        for k4 in range(4):
            kc = vhalf * 4 + k4
            transp(src_t[:, kc * 128:(kc + 1) * 128],
                   bank[:, k4 * 128:(k4 + 1) * 128], ident[:],
                   [src_b, ident_b], [pbuf[bank_i]])
        off = (vhalf * 4) * D + vjt * 128
        split_evac(bank[:].rearrange("p (a b) -> p a b", b=128),
                   pbuf[bank_i],
                   strided(wv_hi[:], off, [[D, 4], [1, 128]]),
                   strided(wv_lo[:], off, [[D, 4], [1, 128]]),
                   [wv_fm_b])
    weight_to_fm(w_in["wq"], [wraw[0][:], wraw[1][:]], wraw_b, wfp, wfp_b,
                 evac_scale=0.125, queue="sp")
    rope_fold(wfp, wfp_b)
    for vjt in range(2):
        for vhalf in range(2):
            wv_block(qsc_s[vjt][:], qsc_sb[vjt], vjt, vhalf, 2 + vhalf)
    split_weight(wfp, wfp_b, wq_hi, wq_lo, wq_fm_b)
    weight_to_fm(w_in["wk"], [qsc_s[2][:], qsc_s[3][:]],
                 [qsc_sb[2], qsc_sb[3]], wfp, wfp_b, queue="gp", n_pre=2)
    rope_fold(wfp, wfp_b)
    split_weight(wfp, wfp_b, wk_hi, wk_lo, wk_fm_b)

    def rope_bias(n, scale_q):
        b_sl = btv(n, 0, 1, 0, D)
        tt("dve", wtv(0, 0, 1, 0, D), b_sl, x_tm[0:1, 0:D], ALU.mult,
           [bias_tm_b[n], cosr_b], [wtmp_b[0]])
        tt("dve", pair3(wtv(1, 0, 1, 0, D)), swap_ap(b_sl, D),
           pair3(x_tm[0:1, D:2 * D]), ALU.mult,
           [bias_tm_b[n], sinr_b], [wtmp_b[1]])
        tt("dve", b_sl, wtv(0, 0, 1, 0, D), wtv(1, 0, 1, 0, D), ALU.add,
           [wtmp_b[0], wtmp_b[1]], [bias_tm_b[n]])
        if scale_q:
            ts1("dve", b_sl, b_sl, 0.125, ALU.mult,
                [bias_tm_b[n]], [bias_tm_b[n]])

    rope_bias("bq", True)
    rope_bias("bk", False)

    def bias_to_fm(n):
        for jc in range(8):
            transp(btv(n, 0, 1, jc * 128, (jc + 1) * 128),
                   pb[2][:, jc:jc + 1], ident[0:1, 0:1],
                   [bias_tm_b[n], ident_b], [pbuf[2]])
        vcopy("dve", bias_fm[n][:], pb[2][:, 0:8], [pbuf[2]], [bias_fm_b[n]])

    for n in ("bq", "bk"):
        bias_to_fm(n)

    for n in ("bv", "bo", "bf"):
        dma_load("sp", wtv(0, 0, 1, 0, D), b_in[n], [], [wtmp_b[0]])
        for jc in range(8):
            transp(wtv(0, 0, 1, jc * 128, (jc + 1) * 128),
                   pb[2][:, jc:jc + 1], ident[0:1, 0:1],
                   [wtmp_b[0], ident_b], [pbuf[2]])
        vcopy("dve", bias_fm[n][:], pb[2][:, 0:8], [pbuf[2]],
              [bias_fm_b[n]])

    x_tm_b = [Buf(f"x_tm{i}") for i in range(2)]
    x_fm_b = [Buf(f"x_fm{i}") for i in range(4)]

    def x_pre_block(j, half):
        fs = j
        bank = pb[half]
        for k4 in range(4):
            kc = half * 4 + k4
            transp(x_tm[:, j * D + kc * 128: j * D + (kc + 1) * 128],
                   bank[:, k4 * 128:(k4 + 1) * 128], ident[:],
                   [x_tm_b[j], ident_b], [pbuf[half]])
        split_evac(bank[:], pbuf[half],
                   x_hi[:, fs * D + half * 512: fs * D + (half + 1) * 512],
                   x_lo[:, fs * D + half * 512: fs * D + (half + 1) * 512],
                   [x_fm_b[fs], wtmp_b[0], bias_tm_b["bq"]])

    quantB = make_quantize(("wB", None), use_act=True, alias=(wfp, wfp_b))
    wfst_hi = [sb16(f"wfst_hi{i}", [128, 512], ws_scope) for i in range(3)]
    wfst_lo = [sb16(f"wfst_lo{i}", [128, 512], ws_scope) for i in range(3)]
    wfst_b = [Buf(f"wfst{i}") for i in range(3)]
    for sl in range(2):
        dma_load("sp", wraw[sl][:], w_in["wf"][sl * 128:(sl + 1) * 128, :],
                 [], [wraw_b[sl]])
    for jp in range(4):
        stA, stB = (2 * jp) % 3, (2 * jp + 1) % 3
        if jp < 3:
            nA = 2 * jp + 2
            dma_load("sp", wraw[nA % 3][:],
                     w_in["wf"][nA * 128:(nA + 1) * 128, :],
                     [], [wraw_b[nA % 3]])
        sA = quantA.steps(wraw[stA][:], wraw_b[stA], wraw[stA][:],
                          [wraw_b[stA]])
        sB = quantB.steps(wraw[stB][:], wraw_b[stB], wraw[stB][:],
                          [wraw_b[stB]])
        qsteps = [f for ab in zip(sA, sB) for f in ab]
        vblocks = []
        if jp < 3:
            for vsl in range(2):
                vjt = 2 * jp + 2 + vsl
                for vhalf in range(2):
                    vblocks.append(
                        lambda vsl=vsl, vjt=vjt, vhalf=vhalf: wv_block(
                            wvraw[vsl][:], wvraw_b[vsl], vjt, vhalf, vhalf))
        else:
            for j in range(2):
                dma_load("sp", x_tm[:, j * D:(j + 1) * D],
                         xp[j * 128:(j + 1) * 128, :],
                         [], [x_tm_b[j], (cosr_b, sinr_b)[j]])
            for j in range(2):
                for half in range(2):
                    vblocks.append(
                        lambda j=j, half=half: x_pre_block(j, half))
        ins_at = (5, 11, 17, 23)
        for i, f in enumerate(qsteps):
            f()
            if i in ins_at and vblocks:
                vblocks[ins_at.index(i)]()
        if jp < 2:
            for sl in range(2):
                vjt = 2 * jp + 4 + sl
                dma_load("act", wvraw[sl][:],
                         w_in["wv"][vjt * 128:(vjt + 1) * 128, :],
                         [], [wvraw_b[sl]])
        for sl in range(2):
            jt = 2 * jp + sl
            st = jt % 3
            for half in range(2):
                bank = pb[2 + half]
                wi = (jt * 2 + half) % 3
                for k4 in range(4):
                    kc = half * 4 + k4
                    transp(wraw[st][:, kc * 128:(kc + 1) * 128],
                           bank[:, k4 * 128:(k4 + 1) * 128], ident[:],
                           [wraw_b[st], ident_b], [pbuf[2 + half]])
                split_evac(bank[:].rearrange("p (a b) -> p a b", b=128),
                           pbuf[2 + half],
                           wfst_hi[wi][:].rearrange("p (a b) -> p a b",
                                                    b=128),
                           wfst_lo[wi][:].rearrange("p (a b) -> p a b",
                                                    b=128),
                           [wfst_b[wi]])
                for scr, stg in ((wf_scr_hi, wfst_hi[wi]),
                                 (wf_scr_lo, wfst_lo[wi])):
                    dst = bass.AP(scr.tensor,
                                  scr.offset + (half * 4) * D + jt * 128,
                                  [[8 * D, 128], [D, 4], [1, 128]])
                    P.dma("gp", lambda e, d=dst, s=stg: e.dma_start(
                        out=d, in_=s[:].rearrange("p (a b) -> p a b",
                                                  b=128)),
                        reads=[wfst_b[wi]], writes=[wf_scr_b])
        if jp < 3:
            nB = 2 * jp + 3
            dma_load("sp", wraw[nB % 3][:],
                     w_in["wf"][nB * 128:(nB + 1) * 128, :],
                     [], [wraw_b[nB % 3]])

    P.barrier()
    for g in reversed(ws_scope):
        g.__exit__(None, None, None)

    XR, FR, KR, VR, QR = 2, 4, 6, 6, 4
    a_scope = []
    k_fm = sb("k_fm", [128, KR * D], a_scope)
    k_fm_b = [Buf(f"k_fm{i}") for i in range(KR)]
    v_tm = sb("v_tm", [128, VR * VW], a_scope)
    v_tm_b = [Buf(f"v_tm{i}") for i in range(VR)]
    q_fm = sb("q_fm", [128, QR * D], a_scope)
    q_fm_b = [Buf(f"q_fm{i}") for i in range(QR)]
    NPR = 3
    probs = sb("probs", [128, NPR * 512], a_scope)
    probs_b = [Buf(f"probs{i}") for i in range(NPR)]
    ctxF = sb("ctxF", [64, 512], a_scope); ctxF_b = Buf("ctxF")
    zb_sb = sb("zb_sb", [64, 512], a_scope);  zb_sb_b = Buf("zb_sb")
    cxh = sb16("cxh", [64, 2 * D], a_scope)
    cxl = sb16("cxl", [64, 2 * D], a_scope)
    cxs_b = Buf("cxs")
    zr = sb("zr", [1, 512], a_scope);      zr_b = Buf("zr")

    ctx_scr_b = [Buf(f"ctx_scr{t}") for t in range(NT)]

    P.op("dve", lambda e: e.memset(
        strided(v_tm[:], 64, [[VW, VR], [65, H]]), 1.0),
        reads=[], writes=list(v_tm_b))

    def proj2(w_hi, w_lo, fm_w_b, bias_n, src_hi_fn, src_lo_fn, src_bufs,
              dst_ap_fn, dst_bufs, banks=None):
        if banks is None:
            banks = ((pb[2][:, 0:512], pbuf[2]), (pb[3][:, 0:512], pbuf[3]))
        if len(banks) == 2:
            banks = banks + banks
        for half in range(2):
            for bi in range(2):
                jcs = (half * 4 + 2 * bi, half * 4 + 2 * bi + 1)
                bank, bank_b = banks[half * 2 + bi]
                first = True
                for kc in range(8):
                    for ji, jc in enumerate(jcs):
                        wsl = slice(kc * D + jc * 128, kc * D + (jc + 1) * 128)
                        po = bank[:, ji * 256:(ji + 1) * 256]
                        last = kc == 7 and ji == 1
                        mm(w_hi[:, wsl], src_hi_fn(kc), po, first, False,
                           [fm_w_b] + list(src_bufs), [bank_b])
                        mm(w_lo[:, wsl], src_hi_fn(kc), po, False, False,
                           [fm_w_b] + list(src_bufs), [bank_b])
                        mm(w_hi[:, wsl], src_lo_fn(kc), po, False, last,
                           [fm_w_b] + list(src_bufs), [bank_b])
                        first = False
            for bi in range(2):
                jcs = (half * 4 + 2 * bi, half * 4 + 2 * bi + 1)
                bank, bank_b = banks[half * 2 + bi]
                for ji, jc in enumerate(jcs):
                    act(dst_ap_fn(jc), bank[:, ji * 256:(ji + 1) * 256],
                        AF.Identity, bias_fm[bias_n][:, jc:jc + 1], 1.0,
                        [bank_b, bias_fm_b[bias_n]], dst_bufs)

    pending_tail = []

    def flush_tail():
        while pending_tail:
            pending_tail.pop(0)()

    def att_group(tq, g):
        t = tq - 2
        qs = tq % QR
        cbank = pb[g % 2]
        cbank_b = pbuf[g % 2]

        def ctx_mms(kp):
            ky = tq - 2 + kp
            pslot = kp % NPR
            vs = ky % VR
            for h_loc in range(4):
                h = 4 * g + h_loc
                mm(v_tm[:, vs * VW + h * 65: vs * VW + h * 65 + 65],
                   probs[:, pslot * 512 + h_loc * 128:
                         pslot * 512 + (h_loc + 1) * 128],
                   cbank[0:65, h_loc * 128:(h_loc + 1) * 128],
                   kp == 0 and h_loc == 0, kp == 4 and h_loc == 3,
                   [v_tm_b[vs], probs_b[pslot]], [cbank_b])

        if True:
            for kp in range(5):
                ky = tq - 2 + kp
                sbank = psE if kp % 2 == 0 else psO
                sbank_b = psE_b if kp % 2 == 0 else psO_b
                ks = ky % KR
                for h_loc in range(4):
                    ch = 2 * g + h_loc // 2
                    hh = h_loc % 2
                    r0 = hh * 64
                    mm(k_fm[r0:r0 + 64,
                            ks * D + ch * 128: ks * D + (ch + 1) * 128],
                       q_fm[r0:r0 + 64,
                            qs * D + ch * 128: qs * D + (ch + 1) * 128],
                       sbank[:, hh * 512 + (h_loc // 2) * 128:
                             hh * 512 + (h_loc // 2) * 128 + 128],
                       True, True,
                       [k_fm_b[ks], q_fm_b[qs]], [sbank_b],
                       tp=(r0, 0))
                pslot = kp % NPR
                p_ap = probs[:, pslot * 512:(pslot + 1) * 512]
                act(p_ap, strided(sbank[:], 0, [[128, 2], [512, 2], [1, 128]]),
                    AF.Exp, killb[:, ky:ky + 1], 1.0,
                    [sbank_b, killb_b], [probs_b[pslot]])
                if kp == 0 or kp == 4:
                    mbuf, mb = (m0, m0_b) if kp == 0 else (m4, m4_b)
                    msk = strided(mbuf[:], 0, [[0, 4], [1, 128]])
                    tt("dve", p_ap.rearrange("p (a b) -> p a b", b=128),
                       p_ap.rearrange("p (a b) -> p a b", b=128),
                       msk, ALU.mult, [probs_b[pslot], mb], [probs_b[pslot]])
                if kp >= 1:
                    ctx_mms(kp - 1)
            ctx_mms(4)
            if pending_tail:
                pending_tail.pop(0)()
            if pending_tail:
                pending_tail.pop(0)()
            act(zr[:], cbank[64:65, :], AF.Ln, 0.0, 1.0,
                [cbank_b], [zr_b])
            act(zr[:], zr[:], AF.Exp, 0.0, -1.0, [zr_b], [zr_b])

            def tail(g=g, cbank=cbank, cbank_b=cbank_b):
                mm(m4[0:1, 0:64], zr[:], pb[3][0:64, :], True, True,
                   [m4_b, zr_b], [pbuf[3]])
                vcopy("dve", zb_sb[:], pb[3][0:64, :], [pbuf[3]], [zb_sb_b])
                tt("dve", ctxF[:], cbank[0:64, :], zb_sb[:], ALU.mult,
                   [cbank_b, zb_sb_b], [ctxF_b])
            pending_tail.append(tail)

            def tail_split(g=g):
                gsl = slice(g * 512, (g + 1) * 512)
                vcopy("dve", cxh[:, gsl], ctxF[:], [ctxF_b], [cxs_b])
                tt("dve", cxl[:, gsl], ctxF[:], cxh[:, gsl],
                   ALU.subtract, [ctxF_b, cxs_b], [cxs_b])
            pending_tail.append(tail_split)

        if g == 3:
            def spill(t=t):
                for scr, cbuf in ((ctx_scr_hi, cxh), (ctx_scr_lo, cxl)):
                    dst = bass.AP(scr.tensor, scr.offset + t * 128 * D,
                                  [[D, 64], [64 * D, 2], [1, D]])
                    P.dma("gp", lambda e, d=dst, s=cbuf: e.dma_start(
                        out=d, in_=s[:].rearrange("p (a b) -> p a b", b=D)),
                        reads=[cxs_b], writes=[ctx_scr_b[t]])
            pending_tail.append(spill)

    def load_and_transpose(j):
        xs = j % XR
        fs = j % FR
        if j >= 2:
            dma_load("sp", x_tm[:, xs * D:(xs + 1) * D],
                     xp[j * 128:(j + 1) * 128, :], [], [x_tm_b[xs]])
        for half in range(2):
            bank = pb[2 + half]
            for k4 in range(4):
                kc = half * 4 + k4
                transp(x_tm[:, xs * D + kc * 128: xs * D + (kc + 1) * 128],
                       bank[:, k4 * 128:(k4 + 1) * 128], ident[:],
                       [x_tm_b[xs], ident_b], [pbuf[2 + half]])
            split_evac(bank[:], pbuf[2 + half],
                       x_hi[:, fs * D + half * 512: fs * D + (half + 1) * 512],
                       x_lo[:, fs * D + half * 512: fs * D + (half + 1) * 512],
                       [x_fm_b[fs]])

    for p in range(10):
        a = 2 * p
        j = a + 1
        flush_tail()
        chunks = []
        if p < 9:
            chunks.append(lambda a=a: load_and_transpose(a + 2))
            chunks.append(lambda a=a: load_and_transpose(a + 3))
        fsA = a % FR
        ksA = a % KR
        src_hi_fn = lambda kc, fsA=fsA: strided(
            x_hi[:], fsA * D + kc * 128, [[D, 2], [1, 128]])
        src_lo_fn = lambda kc, fsA=fsA: strided(
            x_lo[:], fsA * D + kc * 128, [[D, 2], [1, 128]])
        src_bufs = [x_fm_b[fsA], x_fm_b[(fsA + 1) % FR]]
        proj2(wk_hi, wk_lo, wk_fm_b, "bk", src_hi_fn, src_lo_fn,
              src_bufs,
              lambda jc, ksA=ksA: strided(
                  k_fm[:], ksA * D + jc * 128, [[D, 2], [1, 128]]),
              [k_fm_b[ksA], k_fm_b[(ksA + 1) % KR]])
        qsA = a % QR
        vscr_off = qsA * D
        vscr_bufs = [q_fm_b[qsA], q_fm_b[(qsA + 1) % QR]]
        proj2(wv_hi, wv_lo, wv_fm_b, "bv", src_hi_fn, src_lo_fn,
              src_bufs,
              lambda jc, o=vscr_off: strided(q_fm[:], o + jc * 128,
                                             [[D, 2], [1, 128]]),
              vscr_bufs)
        for ti in range(2):
            vs = (a + ti) % VR
            for half in range(2):
                bank = pb[half]
                for c4 in range(4):
                    ch = half * 4 + c4
                    transp(q_fm[:, vscr_off + ti * D + ch * 128:
                                 vscr_off + ti * D + (ch + 1) * 128],
                           bank[:, c4 * 128:(c4 + 1) * 128], ident[:],
                           [vscr_bufs[ti], ident_b], [pbuf[half]])
                dst = strided(v_tm[:], vs * VW + half * 4 * 130,
                              [[130, 4], [65, 2], [1, 64]])
                vcopy("dve", dst,
                      bank[:].rearrange("p (a c b) -> p a c b", a=4, b=64),
                      [pbuf[half]], [v_tm_b[vs]])
        if 3 <= j <= 17:
            def projq(a=a, qsA=qsA, src_hi_fn=src_hi_fn,
                      src_lo_fn=src_lo_fn, src_bufs=src_bufs):
                proj2(wq_hi, wq_lo, wq_fm_b, "bq", src_hi_fn, src_lo_fn,
                      src_bufs,
                      lambda jc, qsA=qsA: strided(
                          q_fm[:], qsA * D + jc * 128, [[D, 2], [1, 128]]),
                      [q_fm_b[qsA], q_fm_b[(qsA + 1) % QR]])
            chunks.append(projq)
        units = [(tq, g) for tq in (a - 2, a - 1) if 2 <= tq <= 17
                 for g in range(4)]
        if not units:
            for ch in chunks:
                ch()
        else:
            ci = 0
            for ui, (tq, g) in enumerate(units):
                att_group(tq, g)
                if ui % 2 == 1 and ci < len(chunks):
                    chunks[ci]()
                    ci += 1
            while ci < len(chunks):
                chunks[ci]()
                ci += 1

    flush_tail()

    P.barrier()
    for g in reversed(a_scope):
        g.__exit__(None, None, None)
    for g in reversed(xa_scope):
        g.__exit__(None, None, None)
    for g in reversed(w_scope):
        g.__exit__(None, None, None)

    b_scope = []
    wo_hi = sb16("wo_hi", [128, 8 * D], b_scope)
    wo_lo = sb16("wo_lo", [128, 8 * D], b_scope)
    wf_hi = sb16("wf_hi", [128, 8 * D], b_scope)
    wf_lo = sb16("wf_lo", [128, 8 * D], b_scope)
    wo_fm_b = Buf("wo_fm");  wf_fm_b = Buf("wf_fm")
    braw_b = [Buf(f"braw{i}") for i in range(2)]



    quantize = make_quantize(("b", b_scope))

    NBS = 3
    ctx_hi = sb16("ctx_hi", [128, NBS * 2 * D], b_scope)
    ctx_lo = sb16("ctx_lo", [128, NBS * 2 * D], b_scope)
    ctxs_b = [Buf(f"ctxs{i}") for i in range(NBS)]
    xres = sb("xres", [128, 2 * 2 * D], b_scope)
    xres_b = [Buf(f"xres{i}") for i in range(2)]
    afm = sb("afm", [128, NBS * 2 * D], b_scope)
    afm_b = [Buf(f"afm{i}") for i in range(NBS)]
    z1 = sb("z1", [128, 4 * D], b_scope)
    z1_b = [Buf(f"z1_{i}") for i in range(4)]
    xdt = sb("xdt", [128, 2 * D], b_scope)
    xdt_b = [Buf(f"xdt{i}") for i in range(2)]
    xdf_hi = sb16("xdf_hi", [128, 2 * D], b_scope)
    xdf_lo = sb16("xdf_lo", [128, 2 * D], b_scope)
    xdf_b = Buf("xdf")
    ffm_b = Buf("ffm")
    outs = sb("outs", [128, 2 * D], b_scope)
    outs_b = [Buf(f"outs{i}") for i in range(2)]
    ln_sum = sb("ln_sum", [128, 4], b_scope)
    ln_sq = sb("ln_sq", [128, 4], b_scope)
    ln_m = sb("ln_m", [128, 4], b_scope)
    ln_v = sb("ln_v", [128, 4], b_scope)
    ln_r = sb("ln_r", [128, 4], b_scope)
    lnsc = sb("lnsc", [128, 2 * D], b_scope)
    _lnsc_shared = [Buf("lnsc0"), Buf("lnsc1")]
    ln_bufs = [{**{n: Buf(n + str(i)) for n in
                   ("ln_sum", "ln_sq", "ln_m", "ln_v", "ln_r")},
                "lnsc": _lnsc_shared[i % 2]}
               for i in range(4)]
    out_hbm_b = [Buf(f"out{t}") for t in range(NT)]

    weight_to_fm(w_in["wo"], [xdt[:, 0:D], xdt[:, D:2 * D]], braw_b,
                 wo_hi, wo_fm_b, lo_t=wo_lo, queue="gp")
    dma_load("act", wf_hi[:], wf_scr_hi, [wf_scr_b], [wf_fm_b])
    dma_load("act", wf_lo[:], wf_scr_lo, [wf_scr_b], [wf_fm_b])

    def layernorm_steps(src_ap, src_b, dst_ap, dst_b, sl=0):
        B = ln_bufs[sl]
        l_sum = ln_sum[:, sl:sl + 1]
        l_sq = ln_sq[:, sl:sl + 1]
        l_m = ln_m[:, sl:sl + 1]
        l_v = ln_v[:, sl:sl + 1]
        l_r = ln_r[:, sl:sl + 1]
        l_sc = lnsc[:, (sl % 2) * D:(sl % 2 + 1) * D]
        st = []
        st.append(lambda: act(l_sc, src_ap, AF.Identity, 0.0, 1.0, [src_b],
                              [B["lnsc"], B["ln_sum"]], accum=l_sum))
        st.append(lambda: act(l_sc, src_ap, AF.Square, 0.0, 1.0, [src_b],
                              [B["lnsc"], B["ln_sq"]], accum=l_sq))
        st.append(lambda: ts1("dve", l_m, l_sum, 1.0 / D, ALU.mult,
                              [B["ln_sum"]], [B["ln_m"]]))
        st.append(lambda: tt("dve", l_v, l_m, l_m, ALU.mult,
                             [B["ln_m"]], [B["ln_v"]]))
        st.append(lambda: P.op(
            "dve", lambda e, o=l_v, i=l_sq, m=l_v:
            e.scalar_tensor_tensor(o, i, 1.0 / D, m, ALU.mult, ALU.subtract),
            reads=[B["ln_sq"], B["ln_v"]], writes=[B["ln_v"]]))
        st.append(lambda: act(l_v, l_v, AF.Sqrt, ceps[:], 1.0,
                              [B["ln_v"], ceps_b], [B["ln_v"]]))
        st.append(lambda: vrecip(l_r, l_v, [B["ln_v"]], [B["ln_r"]]))
        rbc = bass.AP(l_r.tensor, l_r.offset,
                      [list(l_r.ap[0]), [0, src_ap.free_size()]])
        st.append(lambda: P.op(
            "dve", lambda e, o=dst_ap, i=src_ap, m=l_m, r=rbc:
            e.scalar_tensor_tensor(o, i, m, r, ALU.subtract, ALU.mult),
            reads=[src_b, B["ln_m"], B["ln_r"]], writes=list(dst_b)))
        return st

    def layernorm(src_ap, src_b, dst_ap, dst_b, sl=0):
        for f in layernorm_steps(src_ap, src_b, dst_ap, dst_b, sl=sl):
            f()

    def b_ctx_loads(gi):
        cs = gi % NBS
        coff = cs * 2 * D
        for ti in range(2):
            t = 2 * gi + ti
            for scr, dstt in ((ctx_scr_hi, ctx_hi), (ctx_scr_lo, ctx_lo)):
                dma_load("sp", dstt[:, coff + ti * D: coff + (ti + 1) * D],
                         scr[t * 128:(t + 1) * 128, :],
                         [ctx_scr_b[t]], [ctxs_b[cs]])

    def b_xres_loads(gi):
        gs = gi % 2
        goff = gs * 2 * D
        for ti in range(2):
            t = 2 * gi + ti
            dma_load("act", xres[:, goff + ti * D: goff + (ti + 1) * D],
                     xp[(t + 2) * 128:(t + 3) * 128, :], [], [xres_b[gs]])

    def b_oproj(gi):
        cs = gi % NBS
        coff = cs * 2 * D
        proj2(wo_hi, wo_lo, wo_fm_b, "bo",
              lambda kc, coff=coff: strided(ctx_hi[:], coff + kc * 128,
                                            [[D, 2], [1, 128]]),
              lambda kc, coff=coff: strided(ctx_lo[:], coff + kc * 128,
                                            [[D, 2], [1, 128]]),
              [ctxs_b[cs]],
              lambda jc, cs=cs: strided(afm[:], cs * 2 * D + jc * 128,
                                        [[D, 2], [1, 128]]),
              [afm_b[cs]],
              banks=((pb[2][:, 0:512], pbuf[2]), (pb[3][:, 0:512], pbuf[3]),
                     (psE[:, 0:512], psE_b), (psO[:, 0:512], psO_b)))

    def b_front(gi):
        gs = gi % 2
        cs = gi % NBS
        goff = gs * 2 * D
        for ti in range(2):
            aoff = cs * 2 * D + ti * D
            for half in range(2):
                bank = pb[half]
                for c4 in range(4):
                    jc = half * 4 + c4
                    transp(afm[:, aoff + jc * 128: aoff + (jc + 1) * 128],
                           bank[:, c4 * 128:(c4 + 1) * 128], ident[:],
                           [afm_b[cs], ident_b], [pbuf[half]])
                sl_x = xres[:, goff + ti * D + half * 512:
                            goff + ti * D + (half + 1) * 512]
                tt("dve", sl_x, bank[:], sl_x, ALU.add,
                   [pbuf[half], xres_b[gs]], [xres_b[gs]])
    def b_ln1_quant_steps(gi):
        gs = gi % 2
        goff = gs * 2 * D
        lns = [layernorm_steps(
            xres[:, goff + ti * D: goff + (ti + 1) * D], xres_b[gs],
            z1[:, goff + ti * D: goff + (ti + 1) * D],
            [z1_b[gs * 2 + ti]], sl=gs * 2 + ti) for ti in range(2)]
        st = [f for ab in zip(lns[0], lns[1]) for f in ab]
        for ti in range(2):
            st += quantize.steps(z1[:, goff + ti * D: goff + (ti + 1) * D],
                                 z1_b[gs * 2 + ti],
                                 xdt[:, ti * D:(ti + 1) * D], [xdt_b[ti]])
        return st

    def b_ffn_transp(ti):
        for half in range(2):
            bank = pb[half]
            for c4 in range(4):
                kc = half * 4 + c4
                transp(xdt[:, ti * D + kc * 128: ti * D + (kc + 1) * 128],
                       bank[:, c4 * 128:(c4 + 1) * 128], ident[:],
                       [xdt_b[ti], ident_b], [pbuf[half]])
            split_evac(bank[:], pbuf[half],
                       xdf_hi[:, ti * D + half * 512:
                              ti * D + (half + 1) * 512],
                       xdf_lo[:, ti * D + half * 512:
                              ti * D + (half + 1) * 512],
                       [xdf_b])

    def b_ffn(gi):
        for ti in range(2):
            b_ffn_transp(ti)
        ffm_off = (gi % NBS) * 2 * D
        proj2(wf_hi, wf_lo, wf_fm_b, "bf",
              lambda kc: strided(xdf_hi[:], kc * 128, [[D, 2], [1, 128]]),
              lambda kc: strided(xdf_lo[:], kc * 128, [[D, 2], [1, 128]]),
              [xdf_b],
              lambda jc, o=ffm_off: strided(afm[:], o + jc * 128,
                                            [[D, 2], [1, 128]]),
              [ffm_b],
              banks=((psE[:, 512:1024], psE_b), (psO[:, 512:1024], psO_b),
                     (pb[2][:, 0:512], pbuf[2]), (pb[3][:, 0:512], pbuf[3])))

    def b_tail_steps(gi):
        gs = gi % 2
        goff = gs * 2 * D
        ffm_off = (gi % NBS) * 2 * D
        st = []
        for ti in range(2):
            t = 2 * gi + ti
            for half in range(2):
                def blk(ti=ti, half=half):
                    bank = pb[half]
                    for c4 in range(4):
                        jc = half * 4 + c4
                        transp(afm[:, ffm_off + ti * D + jc * 128:
                                     ffm_off + ti * D + (jc + 1) * 128],
                               bank[:, c4 * 128:(c4 + 1) * 128], ident[:],
                               [ffm_b, ident_b], [pbuf[half]])
                    sl_z = z1[:, goff + ti * D + half * 512:
                              goff + ti * D + (half + 1) * 512]
                    tt("dve", sl_z, bank[:], sl_z, ALU.add,
                       [pbuf[half], z1_b[gs * 2 + ti]],
                       [z1_b[gs * 2 + ti]])
                st.append(blk)
            st += layernorm_steps(z1[:, goff + ti * D: goff + (ti + 1) * D],
                                  z1_b[gs * 2 + ti],
                                  outs[:, ti * D:(ti + 1) * D],
                                  [outs_b[ti]], sl=gs * 2 + ti)
            st.append(lambda t=t, ti=ti: P.dma(
                "gp", lambda e, t=t, ti=ti: e.dma_start(
                    out=out_hbm[t * 128:(t + 1) * 128, :],
                    in_=outs[:, ti * D:(ti + 1) * D]),
                reads=[outs_b[ti]], writes=[out_hbm_b[t]]))
        return st

    b_ctx_loads(0)
    b_xres_loads(0)
    b_oproj(0)
    b_ctx_loads(1)
    b_oproj(1)
    for gi in range(NT // 2):
        b_front(gi)
        sA = b_ln1_quant_steps(gi)
        sB = b_tail_steps(gi - 1) if gi > 0 else []
        for x in range(max(len(sA), len(sB))):
            if x < len(sA):
                sA[x]()
            if x < len(sB):
                sB[x]()
        if gi + 1 < NT // 2:
            b_xres_loads(gi + 1)
        if gi + 2 < NT // 2:
            b_ctx_loads(gi + 2)
            b_oproj(gi + 2)
        b_ffn(gi)
    for f in b_tail_steps(NT // 2 - 1):
        f()

    P.op("gp", None, reads=list(out_hbm_b), writes=[])

    counts = P.finalize()
    sem_keys = list(ENGINES) + [("dma", e, r) for e in ("sp", "gp", "act")
                                for r in range(NROT)]
    sems = {}
    sem_guards = []
    for k in sem_keys:
        nm = k if isinstance(k, str) else f"dma_{k[1]}_{k[2]}"
        g = nc.semaphore(f"sem_{nm}")
        sems[k] = g.__enter__()
        sem_guards.append(g)
    blk = nc.Block()
    block = blk.__enter__()

    @block.tensor
    def _(e):
        P.emit(e, "pe", sems)

    @block.scalar
    def _(e):
        P.emit(e, "act", sems)

    @block.vector
    def _(e):
        P.emit(e, "dve", sems)

    @block.gpsimd
    def _(e):
        P.emit(e, "gp", sems)

    @block.sync
    def _(e):
        P.emit(e, "sp", sems)

    blk.__exit__(None, None, None)
    for g in reversed(sem_guards):
        g.__exit__(None, None, None)
    for g in reversed(b_scope):
        g.__exit__(None, None, None)
    for g in reversed(guards):
        g.__exit__(None, None, None)
    return nc


def make_inputs(L0_attn_x, L0_attn_Wq, L0_attn_bq, L0_attn_Wk, L0_attn_bk,
                L0_attn_Wv, L0_attn_bv, L0_attn_Wo, L0_attn_bo,
                L0_ffn_weight, L0_ffn_bias):
    x = np.asarray(L0_attn_x, dtype=np.float32)
    B, S, _ = x.shape
    per = S // 4
    in_maps = []
    shared = {
        "wq": np.ascontiguousarray(L0_attn_Wq, dtype=np.float32),
        "wk": np.ascontiguousarray(L0_attn_Wk, dtype=np.float32),
        "wv": np.ascontiguousarray(L0_attn_Wv, dtype=np.float32),
        "wo": np.ascontiguousarray(
            np.asarray(L0_attn_Wo, dtype=np.float32)[:, _WO_PERM]),
        "wf": np.ascontiguousarray(L0_ffn_weight, dtype=np.float32),
        "bq": np.ascontiguousarray(np.asarray(L0_attn_bq, dtype=np.float32)
                                   .reshape(1, D)),
        "bk": np.ascontiguousarray(np.asarray(L0_attn_bk, dtype=np.float32)
                                   .reshape(1, D)),
        "bv": np.ascontiguousarray(np.asarray(L0_attn_bv, dtype=np.float32)
                                   .reshape(1, D)),
        "bo": np.ascontiguousarray(np.asarray(L0_attn_bo, dtype=np.float32)
                                   .reshape(1, D)),
        "bf": np.ascontiguousarray(np.asarray(L0_ffn_bias, dtype=np.float32)
                                   .reshape(1, D)),
    }
    for c in range(8):
        b, s = divmod(c, 4)
        lo, hi = s * per, (s + 1) * per
        pad_l = min(256, lo)
        pad_r = min(256, S - hi)
        sl = x[b, lo - pad_l: hi + pad_r]
        xp_arr = np.zeros((NP * T, D), dtype=np.float32)
        xp_arr[256 - pad_l: 256 - pad_l + sl.shape[0]] = sl
        kill = np.full(NP * T, -MHAT, dtype=np.float32)
        if pad_l < 256:
            kill[:256 - pad_l] += KILL
        if pad_r < 256:
            kill[NP * T - (256 - pad_r):] += KILL
        killb = np.ascontiguousarray(
            kill.reshape(NP, T).T)
        m = dict(shared)
        m["xp"] = xp_arr
        m["killb_hbm"] = killb
        in_maps.append(m)
    return in_maps


_NC_CACHE = None


def kernel(L0_attn_x, L0_attn_Wq, L0_attn_bq, L0_attn_Wk, L0_attn_bk,
           L0_attn_Wv, L0_attn_bv, L0_attn_Wo, L0_attn_bo,
           L0_ffn_weight, L0_ffn_bias):
    global _NC_CACHE
    if _NC_CACHE is None:
        _NC_CACHE = build_nc()
    nc = _NC_CACHE
    in_maps = make_inputs(L0_attn_x, L0_attn_Wq, L0_attn_bq, L0_attn_Wk,
                          L0_attn_bk, L0_attn_Wv, L0_attn_bv, L0_attn_Wo,
                          L0_attn_bo, L0_ffn_weight, L0_ffn_bias)
    res = run_bass_kernel_spmd(nc, in_maps, core_ids=list(range(8)))
    x = np.asarray(L0_attn_x)
    B, S, _ = x.shape
    per = S // 4
    out = np.empty((B, S, D), dtype=np.float32)
    for c in range(8):
        b, s = divmod(c, 4)
        out[b, s * per:(s + 1) * per] = res.results[c]["out"]
    return out


# version 59
# speedup vs baseline: 1.0036x; 1.0036x over previous
import numpy as np
import concourse.bass as bass
import concourse.mybir as mybir
from concourse.bass_utils import run_bass_kernel_spmd

F32 = mybir.dt.float32
I32 = mybir.dt.int32
AF = mybir.ActivationFunctionType
ALU = mybir.AluOpType
AX = mybir.AxisListType

D = 1024
H = 16
T = 128
NT = 16
NP = 20
MHAT = 30.0
KILL = -1e9
VW = H * 65

_WO_PERM = np.concatenate(
    [np.concatenate([np.arange(64 * c, 64 * c + 64),
                     np.arange(512 + 64 * c, 512 + 64 * c + 64)])
     for c in range(8)])

ENGINES = ("pe", "act", "dve", "gp", "sp")
NROT = 4


class Buf:
    __slots__ = ("name", "writers", "readers")

    def __init__(self, name):
        self.name = name
        self.writers = {}
        self.readers = {}


class Op:
    __slots__ = ("idx", "eng", "build", "deps", "need_inc", "is_dma",
                 "dma_k", "waits", "sem_val", "sem_key", "hard", "stream")

    def __init__(self, idx, eng, build, is_dma):
        self.idx = idx
        self.eng = eng
        self.build = build
        self.deps = set()
        self.need_inc = False
        self.is_dma = is_dma
        self.dma_k = -1
        self.waits = []
        self.sem_val = 0
        self.sem_key = None
        self.hard = set()
        self.stream = True


class Prog:
    def __init__(self):
        self.ops = []
        self.dma_count = {}
        self.pending_barrier = None
        self._bar_seen = set()
        self._dma_idx = {}

    def _record(self, eng, build, reads, writes, is_dma):
        i = len(self.ops)
        op = Op(i, eng, build, is_dma)
        for b in reads:
            for j in b.writers.values():
                op.deps.add(j)
            b.readers[eng + str(is_dma) + str(i)] = i
        for b in writes:
            for j in b.writers.values():
                op.deps.add(j)
            for j in b.readers.values():
                op.deps.add(j)
            b.writers = {eng: i}
            b.readers = {}
        op.deps.discard(i)
        if self.pending_barrier is not None and eng not in self._bar_seen:
            self._bar_seen.add(eng)
            for j in self.pending_barrier:
                if j != i:
                    op.deps.add(j)
        if is_dma:
            k = self.dma_count.get(eng, 0)
            self.dma_count[eng] = k + 1
            op.dma_k = k
            if k >= NROT:
                prev = self._dma_idx[(eng, k - NROT)]
                self.ops[prev].need_inc = True
                op.deps.add(prev)
            self._dma_idx[(eng, k)] = i
        self.ops.append(op)
        return i

    def op(self, eng, build, reads=(), writes=(), stream=True):
        i = self._record(eng, build, reads, writes, False)
        self.ops[i].hard = getattr(self.ops[i], "hard", set())
        if not stream:
            self.ops[i].stream = False
        return i

    def hard_dep(self, i, j):
        self.ops[i].hard = getattr(self.ops[i], "hard", set()) | {j}

    def dma(self, eng, build, reads=(), writes=()):
        return self._record(eng, build, reads, writes, True)

    def barrier(self):
        last = {}
        for o in self.ops:
            if o.is_dma:
                last[("dma", o.eng, o.dma_k % NROT)] = o.idx
            else:
                last[o.eng] = o.idx
        idxs = sorted(set(last.values()))
        for j in idxs:
            self.ops[j].need_inc = True
        self.pending_barrier = idxs
        self._bar_seen = set()

    def finalize(self):
        for o in self.ops:
            if o.eng == "pe" and o.stream:
                o.deps = {j for j in o.deps if self.ops[j].eng != "pe"
                          or self.ops[j].is_dma or not self.ops[j].stream}
                o.deps |= o.hard
        for o in self.ops:
            for j in o.deps:
                self.ops[j].need_inc = True
        counts = {}
        for o in self.ops:
            if o.is_dma:
                key = ("dma", o.eng, o.dma_k % NROT)
                amt = 16
            else:
                key = o.eng
                amt = 1
            o.sem_key = key
            if o.need_inc:
                counts[key] = counts.get(key, 0) + amt
                o.sem_val = counts[key]
        marks = {e: {} for e in ENGINES}
        for o in self.ops:
            need = {}
            for j in o.deps:
                p = self.ops[j]
                k, v = p.sem_key, p.sem_val
                if need.get(k, 0) < v:
                    need[k] = v
            m = marks[o.eng]
            for k in sorted(need, key=str):
                v = need[k]
                if m.get(k, 0) < v:
                    o.waits.append((k, v))
                    m[k] = v
        return counts

    def emit(self, eng_obj, eng_name, sems):
        for o in self.ops:
            if o.eng != eng_name:
                continue
            for k, v in o.waits:
                eng_obj.wait_ge(sems[k], v)
            if o.build is None:
                continue
            inst = o.build(eng_obj)
            if o.need_inc:
                inst.then_inc(sems[o.sem_key], 16 if o.is_dma else 1)


def _rope_vectors():
    HD = 64
    inv = 1.0 / (10000.0 ** (np.arange(0, HD, 2, dtype=np.float64) / HD))
    freqs = np.arange(H, dtype=np.float64)[:, None] * inv[None, :]
    emb = np.concatenate([freqs, freqs], axis=-1)
    cos = np.cos(emb).astype(np.float32).reshape(-1)
    sin = np.sin(emb).astype(np.float32).reshape(-1)
    sgn = np.where(np.arange(D) % 2 == 0, -1.0, 1.0).astype(np.float32)
    return cos, sin * sgn


def build_nc():
    nc = bass.Bass("TRN2", target_bir_lowering=False)

    xp = nc.dram_tensor("xp", [NP * T, D], F32, kind="ExternalInput").ap()
    w_in = {}
    for nm in ("wq", "wk", "wv", "wo", "wf"):
        w_in[nm] = nc.dram_tensor(nm, [D, D], F32, kind="ExternalInput").ap()
    b_in = {}
    for nm in ("bq", "bk", "bv", "bo", "bf"):
        b_in[nm] = nc.dram_tensor(nm, [1, D], F32, kind="ExternalInput").ap()
    killb_in = nc.dram_tensor("killb_hbm", [128, NP], F32, kind="ExternalInput").ap()
    out_hbm = nc.dram_tensor("out", [NT * T, D], F32, kind="ExternalOutput").ap()
    wf_scr_hi = nc.dram_tensor("wf_scr_hi", [128, 8 * D],
                               mybir.dt.bfloat16, kind="Internal").ap()
    wf_scr_lo = nc.dram_tensor("wf_scr_lo", [128, 8 * D],
                               mybir.dt.bfloat16, kind="Internal").ap()
    ctx_scr_hi = nc.dram_tensor("ctx_scr_hi", [NT * T, D],
                                mybir.dt.bfloat16, kind="Internal").ap()
    ctx_scr_lo = nc.dram_tensor("ctx_scr_lo", [NT * T, D],
                                mybir.dt.bfloat16, kind="Internal").ap()

    cosv, sinv = _rope_vectors()
    cos_c = nc.inline_tensor(np.repeat(cosv[None, :], 128, 0), "cos_c").ap()
    sin_c = nc.inline_tensor(np.repeat(sinv[None, :], 128, 0), "sin_c").ap()
    ident_c = nc.inline_tensor(np.eye(128, dtype=np.float32), "ident_c").ap()
    ii, jj = np.indices((128, 128))
    m0_c = nc.inline_tensor((ii >= jj).astype(np.float32), "m0_c").ap()
    c192_c = nc.inline_tensor(np.full((128, 1), 192.0, np.float32),
                              "c192_c").ap()
    c8_c = nc.inline_tensor(np.full((128, 1), 8.0, np.float32), "c8_c").ap()
    ceps_c = nc.inline_tensor(np.full((128, 1), 1e-5, np.float32),
                              "ceps_c").ap()
    m4_c = nc.inline_tensor((ii <= jj).astype(np.float32), "m4_c").ap()

    P = Prog()
    guards = []

    def sb(name, shape, scope=None):
        g = nc.sbuf_tensor(name, shape, F32)
        t = g.__enter__()
        (guards if scope is None else scope).append(g)
        return t

    pb = []
    for i in range(4):
        g = nc.psum_tensor(f"pb{i}", [128, 512], F32)
        pb.append(g.__enter__())
        guards.append(g)
    pbuf = [Buf(f"pb{i}") for i in range(4)]
    g = nc.psum_tensor("psE", [128, 1024], F32)
    psE = g.__enter__(); guards.append(g)
    g = nc.psum_tensor("psO", [128, 1024], F32)
    psO = g.__enter__(); guards.append(g)
    psE_b = Buf("psE"); psO_b = Buf("psO")

    ident = sb("ident", [128, 128]);   ident_b = Buf("ident")
    m0 = sb("m0", [128, 128]);         m0_b = Buf("m0")
    m4 = sb("m4", [128, 128]);         m4_b = Buf("m4")
    killb = sb("killb_sb", [128, NP]);  killb_b = Buf("killb")
    c192 = sb("c192", [128, 1]);       c192_b = Buf("c192")
    c8 = sb("c8", [128, 1]);           c8_b = Buf("c8")
    ceps = sb("ceps", [128, 1]);       ceps_b = Buf("ceps")
    bias_fm = {n: sb("fm_" + n, [128, 8]) for n in b_in}
    bias_fm_b = {n: Buf("fm_" + n) for n in b_in}
    wf_scr_b = Buf("wf_scr")

    def dma_load(queue, dst_ap, src_ap, reads, writes):
        P.dma(queue, lambda e, d=dst_ap, s=src_ap: e.dma_start(out=d, in_=s),
              reads=list(reads), writes=list(writes))

    def mm(lhsT, rhs, o, start, stop, reads, writes, tp=None, stream=True):
        P.op("pe", lambda e, a=lhsT, b=rhs, c=o, s1=start, s2=stop, t=tp:
             e.matmul(c, a, b, start=s1, stop=s2) if t is None else
             e.matmul(c, a, b, start=s1, stop=s2, tile_position=t),
             reads=list(reads), writes=list(writes), stream=stream)

    def transp(in_ap, out_ap, idn, reads, writes):
        P.op("pe", lambda e, a=in_ap, o=out_ap, i=idn: e.transpose(o, a, i),
             reads=list(reads), writes=list(writes))

    def act(out_ap, in_ap, func, bias, scale, reads, writes, accum=None):
        P.op("act", lambda e, o=out_ap, i=in_ap, f=func, b=bias, s=scale,
             a=accum: e.activation(o, i, f, bias=b, scale=s, accum_out=a),
             reads=list(reads), writes=list(writes))

    def tt(eng, out_ap, a_ap, b_ap, opk, reads, writes):
        P.op(eng, lambda e, o=out_ap, a=a_ap, b=b_ap, k=opk:
             e.tensor_tensor(o, a, b, op=k),
             reads=list(reads), writes=list(writes))

    def ts2(eng, out_ap, in_ap, s1, s2, op1, op2, reads, writes):
        P.op(eng, lambda e, o=out_ap, i=in_ap, a=s1, b=s2, k1=op1, k2=op2:
             e.tensor_scalar(o, i, a, b, k1, k2),
             reads=list(reads), writes=list(writes))

    def ts1(eng, out_ap, in_ap, s1, op1, reads, writes):
        P.op(eng, lambda e, o=out_ap, i=in_ap, a=s1, k1=op1:
             e.tensor_single_scalar(o, i, a, k1),
             reads=list(reads), writes=list(writes))

    def vcopy(eng, out_ap, in_ap, reads, writes):
        P.op(eng, lambda e, o=out_ap, i=in_ap: e.tensor_copy(o, i),
             reads=list(reads), writes=list(writes))

    def vreduce(out_ap, in_ap, opk, reads, writes, absval=False):
        P.op("dve", lambda e, o=out_ap, i=in_ap, k=opk, av=absval:
             e.tensor_reduce(o, i, axis=AX.X, op=k, apply_absolute_value=av),
             reads=list(reads), writes=list(writes))

    def vrecip(out_ap, in_ap, reads, writes):
        P.op("dve", lambda e, o=out_ap, i=in_ap: e.reciprocal(o, i),
             reads=list(reads), writes=list(writes))

    def split_evac(psum_ap, psum_buf, hi_ap, lo_ap, wbufs):
        act(hi_ap, psum_ap, AF.Identity, 0.0, 1.0, [psum_buf], list(wbufs))
        tt("dve", lo_ap, psum_ap, hi_ap, ALU.subtract,
           [psum_buf] + list(wbufs), list(wbufs))

    def strided(t_ap, off, dims):
        return bass.AP(t_ap.tensor, t_ap.offset + off,
                       [list(t_ap.ap[0])] + [list(d) for d in dims])

    dma_load("sp", ident[:], ident_c, [], [ident_b])
    dma_load("sp", m0[:], m0_c, [], [m0_b])
    dma_load("sp", m4[:], m4_c, [], [m4_b])
    dma_load("sp", killb[:], killb_in, [], [killb_b])
    dma_load("sp", c192[:], c192_c, [], [c192_b])
    dma_load("sp", c8[:], c8_c, [], [c8_b])
    dma_load("sp", ceps[:], ceps_c, [], [ceps_b])

    def weight_to_fm(w_hbm, wraw_ap, wraw_b, fm_t, fm_b, lo_t=None,
                     evac_scale=1.0, queue="sp", n_pre=0, queue2=None):
        for jt in range(8):
            sl = jt % 2
            if jt >= n_pre:
                q = queue if (queue2 is None or jt % 2 == 0) else queue2
                dma_load(q, wraw_ap[sl],
                         w_hbm[jt * 128:(jt + 1) * 128, :],
                         [], [wraw_b[sl]])
            for half in range(2):
                bank = pb[half]
                for k4 in range(4):
                    kc = half * 4 + k4
                    transp(wraw_ap[sl][:, kc * 128:(kc + 1) * 128],
                           bank[:, k4 * 128:(k4 + 1) * 128], ident[:],
                           [wraw_b[sl], ident_b], [pbuf[half]])
                off = (half * 4) * D + jt * 128
                src3 = bank[:].rearrange("p (a b) -> p a b", b=128)
                if lo_t is None:
                    act(strided(fm_t[:], off, [[D, 4], [1, 128]]), src3,
                        AF.Identity, 0.0, evac_scale, [pbuf[half]], [fm_b])
                else:
                    split_evac(src3, pbuf[half],
                               strided(fm_t[:], off, [[D, 4], [1, 128]]),
                               strided(lo_t[:], off, [[D, 4], [1, 128]]),
                               [fm_b])

    def swap_ap(t_ap, ncols):
        return bass.AP(t_ap.tensor, t_ap.offset + 1,
                       [list(t_ap.ap[0]), [2, ncols // 2], [-1, 2]])

    def pair3(t_ap):
        return t_ap.rearrange("p (a b) -> p a b", b=2)

    def make_quantize(scope, W=D, use_act=False, alias=None):
        NB32 = W // 32
        if alias is None:
            qs_bm = sb("qsW_bm" + scope[0], [128, NB32], scope[1])
            qs_sc = sb("qsW_sc" + scope[0], [128, NB32], scope[1])
            qs_rs = sb("qsW_rs" + scope[0], [128, NB32], scope[1])
            qsc = [sb(f"qscW{i}" + scope[0], [128, W], scope[1])
                   for i in range(6)]
            qbf = sb16("qbfW" + scope[0], [128, W], scope[1])
            qs_bm_b = Buf("qs_bm" + scope[0])
            qs_sc_b = Buf("qs_sc" + scope[0])
            qs_rs_b = Buf("qs_rs" + scope[0])
            qsc_b = [Buf(f"qsc{i}" + scope[0]) for i in range(6)]
            qbf_b = Buf("qbf" + scope[0])
        else:
            host, hb = alias
            qsc = [host[:, i * W:(i + 1) * W] for i in range(6)]
            qbf = host[:, 6 * W:6 * W + W // 2].bitcast(mybir.dt.bfloat16)
            qs_bm = host[:, 13 * W // 2:13 * W // 2 + NB32]
            qs_sc = host[:, 13 * W // 2 + NB32:13 * W // 2 + 2 * NB32]
            qs_rs = host[:, 13 * W // 2 + 2 * NB32:13 * W // 2 + 3 * NB32]
            qs_bm_b = qs_sc_b = qs_rs_b = qbf_b = hb
            qsc_b = [hb] * 6

        def quant_steps(src_ap, src_b, dst_ap, dst_b):
            s3 = lambda a: a.rearrange("p (b k) -> p b k", k=32)
            bcast = lambda a: a.unsqueeze(2).broadcast_to([128, NB32, 32])
            Tq, Aq, G2, G4, Sq, Rq = qsc
            Tq_b, Aq_b, G2_b, G4_b, Sq_b, Rq_b = qsc_b
            st = []
            st.append(lambda: vreduce(
                qs_bm[:].rearrange("p (b o) -> p b o", o=1),
                s3(src_ap), ALU.max, [src_b], [qs_bm_b], absval=True))
            st.append(lambda: ts2(
                "dve", qs_sc[:], qs_bm[:], 1.0 / 7.5, 1e-30, ALU.mult,
                ALU.max, [qs_bm_b], [qs_sc_b]))
            st.append(lambda: vrecip(qs_rs[:], qs_sc[:],
                                     [qs_sc_b], [qs_rs_b]))
            st.append(lambda: tt(
                "dve", s3(Tq[:]), s3(src_ap), bcast(qs_rs[:]), ALU.mult,
                [src_b, qs_rs_b], [Tq_b]))
            if use_act:
                st.append(lambda: act(Aq[:], Tq[:], AF.Abs, 0.0, 1.0,
                                      [Tq_b], [Aq_b]))
                thr2, thr4 = 2.0, 4.0
            else:
                st.append(lambda: tt("dve", Aq[:], Tq[:], Tq[:], ALU.mult,
                                     [Tq_b], [Aq_b]))
                thr2, thr4 = 4.0, 16.0
            st.append(lambda: ts1("dve", G2[:], Aq[:], thr2, ALU.is_ge,
                                  [Aq_b], [G2_b]))
            st.append(lambda: ts1("dve", G4[:], Aq[:], thr4, ALU.is_ge,
                                  [Aq_b], [G4_b]))
            st.append(lambda: P.op(
                "dve", lambda e, o=Rq[:], a=G2[:], b=G4[:]:
                e.scalar_tensor_tensor(o, a, 2.0, b, ALU.mult, ALU.add),
                reads=[G2_b, G4_b], writes=[Rq_b]))
            if use_act:
                st.append(lambda: act(Rq[:], Rq[:], AF.Identity, c8[:],
                                      -2.0, [Rq_b, c8_b], [Rq_b]))
            else:
                st.append(lambda: ts2("dve", Rq[:], Rq[:], -2.0, 8.0,
                                      ALU.mult, ALU.add, [Rq_b], [Rq_b]))
            st.append(lambda: ts2(
                "dve", Sq[:].bitcast(I32), Rq[:].bitcast(I32),
                -1, 0x7F000000, ALU.mult, ALU.add, [Rq_b], [Sq_b]))
            st.append(lambda: tt("dve", Aq[:], Tq[:], Rq[:], ALU.mult,
                                 [Tq_b, Rq_b], [Aq_b]))
            if use_act:
                st.append(lambda: act(qbf[:], Aq[:], AF.Identity, c192[:],
                                      1.0, [Aq_b, c192_b], [qbf_b]))
            else:
                st.append(lambda: ts1("dve", qbf[:], Aq[:], 192.0, ALU.add,
                                      [Aq_b], [qbf_b]))
            st.append(lambda: P.op(
                "dve", lambda e, o=Tq[:], i=qbf[:], s=Sq[:]:
                e.scalar_tensor_tensor(o, i, -192.0, s, ALU.add, ALU.mult),
                reads=[qbf_b, Sq_b], writes=[Tq_b]))
            st.append(lambda: tt(
                "dve", s3(dst_ap), s3(Tq[:]), bcast(qs_sc[:]), ALU.mult,
                [Tq_b, qs_sc_b], list(dst_b)))
            return st

        def quantize(src_ap, src_b, dst_ap, dst_b):
            for f in quant_steps(src_ap, src_b, dst_ap, dst_b):
                f()
        quantize.steps = quant_steps
        quantize.scratch = (qsc, qsc_b)
        return quantize

    w_scope = []
    ws_scope = []

    def sb16(name, shape, scope):
        g = nc.sbuf_tensor(name, shape, mybir.dt.bfloat16)
        t = g.__enter__()
        scope.append(g)
        return t

    wq_hi = sb16("wq_hi", [128, 8 * D], w_scope)
    wq_lo = sb16("wq_lo", [128, 8 * D], w_scope)
    wk_hi = sb16("wk_hi", [128, 8 * D], w_scope)
    wk_lo = sb16("wk_lo", [128, 8 * D], w_scope)
    wv_hi = sb16("wv_hi", [128, 8 * D], w_scope)
    wv_lo = sb16("wv_lo", [128, 8 * D], w_scope)
    wq_fm_b = Buf("wq_fm"); wk_fm_b = Buf("wk_fm"); wv_fm_b = Buf("wv_fm")
    xa_scope = []
    x_tm = sb("x_tm", [128, 2 * D], xa_scope)
    x_hi = sb16("x_hi", [128, 4 * D], xa_scope)
    x_lo = sb16("x_lo", [128, 4 * D], xa_scope)
    cosr = x_tm[:, 0:D];       cosr_b = Buf("cosr")
    sinr = x_tm[:, D:2 * D];   sinr_b = Buf("sinr")

    def wtv(i, r0, r1, c0, c1):
        return x_hi[r0:r1, 2 * (i * D + c0):2 * (i * D + c1)].bitcast(F32)

    def btv(n, r0, r1, c0, c1):
        base = 0 if n == "bq" else D
        return x_lo[r0:r1, 2 * (base + c0):2 * (base + c1)].bitcast(F32)
    wtmp_b = [Buf(f"wtmp{i}") for i in range(2)]
    bias_tm_b = {n: Buf("tm_" + n) for n in ("bq", "bk")}
    wraw = [sb(f"wraw{i}", [128, D], ws_scope) for i in range(3)]
    wraw_b = [Buf(f"wraw{i}") for i in range(3)]
    wfp = sb("wfp", [128, 8 * D], ws_scope);  wfp_b = Buf("wfp")
    wvraw = [sb(f"wvraw{i}", [128, D], ws_scope) for i in range(2)]
    wvraw_b = [Buf(f"wvraw{i}") for i in range(2)]
    for n in ("bq", "bk"):
        dma_load("act", btv(n, 0, 1, 0, D), b_in[n], [], [bias_tm_b[n]])


    dma_load("act", cosr, cos_c, [], [cosr_b])
    dma_load("act", sinr, sin_c, [], [sinr_b])

    def split_weight(fp_t, fp_b, hi_t, lo_t, fm_b):
        for kc in range(8):
            c = fp_t[:, kc * D:(kc + 1) * D]
            act(hi_t[:, kc * D:(kc + 1) * D], c, AF.Identity, 0.0, 1.0,
                [fp_b], [fm_b])
            tt("dve", lo_t[:, kc * D:(kc + 1) * D], c,
               hi_t[:, kc * D:(kc + 1) * D], ALU.subtract,
               [fp_b, fm_b], [fm_b])

    def rope_fold(fm_t, fm_b):
        for kc in range(8):
            w_sl = fm_t[:, kc * D:(kc + 1) * D]
            tt("dve", wtv(0, 0, 128, 0, D), w_sl, cosr, ALU.mult,
               [fm_b, cosr_b], [wtmp_b[0]])
            tt("dve", pair3(wtv(1, 0, 128, 0, D)), swap_ap(w_sl, D),
               pair3(sinr), ALU.mult, [fm_b, sinr_b], [wtmp_b[1]])
            tt("dve", w_sl, wtv(0, 0, 128, 0, D), wtv(1, 0, 128, 0, D),
               ALU.add, [wtmp_b[0], wtmp_b[1]], [fm_b])

    quantA = make_quantize(("w", ws_scope), use_act=True)
    qsc_s, qsc_sb = quantA.scratch
    for sl in range(2):
        dma_load("act", qsc_s[sl][:],
                 w_in["wv"][sl * 128:(sl + 1) * 128, :], [], [qsc_sb[sl]])
        dma_load("act", wvraw[sl][:],
                 w_in["wv"][(2 + sl) * 128:(3 + sl) * 128, :],
                 [], [wvraw_b[sl]])
        dma_load("gp", qsc_s[2 + sl][:],
                 w_in["wk"][sl * 128:(sl + 1) * 128, :],
                 [], [qsc_sb[2 + sl]])

    def wv_block(src_t, src_b, vjt, vhalf, bank_i):
        bank = pb[bank_i]
        for k4 in range(4):
            kc = vhalf * 4 + k4
            transp(src_t[:, kc * 128:(kc + 1) * 128],
                   bank[:, k4 * 128:(k4 + 1) * 128], ident[:],
                   [src_b, ident_b], [pbuf[bank_i]])
        off = (vhalf * 4) * D + vjt * 128
        split_evac(bank[:].rearrange("p (a b) -> p a b", b=128),
                   pbuf[bank_i],
                   strided(wv_hi[:], off, [[D, 4], [1, 128]]),
                   strided(wv_lo[:], off, [[D, 4], [1, 128]]),
                   [wv_fm_b])
    weight_to_fm(w_in["wq"], [wraw[0][:], wraw[1][:]], wraw_b, wfp, wfp_b,
                 evac_scale=0.125, queue="sp", queue2="gp")
    rope_fold(wfp, wfp_b)
    for vjt in range(2):
        for vhalf in range(2):
            wv_block(qsc_s[vjt][:], qsc_sb[vjt], vjt, vhalf, 2 + vhalf)
    split_weight(wfp, wfp_b, wq_hi, wq_lo, wq_fm_b)
    weight_to_fm(w_in["wk"], [qsc_s[2][:], qsc_s[3][:]],
                 [qsc_sb[2], qsc_sb[3]], wfp, wfp_b, queue="gp", n_pre=2)
    rope_fold(wfp, wfp_b)
    split_weight(wfp, wfp_b, wk_hi, wk_lo, wk_fm_b)

    def rope_bias(n, scale_q):
        b_sl = btv(n, 0, 1, 0, D)
        tt("dve", wtv(0, 0, 1, 0, D), b_sl, x_tm[0:1, 0:D], ALU.mult,
           [bias_tm_b[n], cosr_b], [wtmp_b[0]])
        tt("dve", pair3(wtv(1, 0, 1, 0, D)), swap_ap(b_sl, D),
           pair3(x_tm[0:1, D:2 * D]), ALU.mult,
           [bias_tm_b[n], sinr_b], [wtmp_b[1]])
        tt("dve", b_sl, wtv(0, 0, 1, 0, D), wtv(1, 0, 1, 0, D), ALU.add,
           [wtmp_b[0], wtmp_b[1]], [bias_tm_b[n]])
        if scale_q:
            ts1("dve", b_sl, b_sl, 0.125, ALU.mult,
                [bias_tm_b[n]], [bias_tm_b[n]])

    rope_bias("bq", True)
    rope_bias("bk", False)

    def bias_to_fm(n):
        for jc in range(8):
            transp(btv(n, 0, 1, jc * 128, (jc + 1) * 128),
                   pb[2][:, jc:jc + 1], ident[0:1, 0:1],
                   [bias_tm_b[n], ident_b], [pbuf[2]])
        vcopy("dve", bias_fm[n][:], pb[2][:, 0:8], [pbuf[2]], [bias_fm_b[n]])

    for n in ("bq", "bk"):
        bias_to_fm(n)

    for n in ("bv", "bo", "bf"):
        dma_load("sp", wtv(0, 0, 1, 0, D), b_in[n], [], [wtmp_b[0]])
        for jc in range(8):
            transp(wtv(0, 0, 1, jc * 128, (jc + 1) * 128),
                   pb[2][:, jc:jc + 1], ident[0:1, 0:1],
                   [wtmp_b[0], ident_b], [pbuf[2]])
        vcopy("dve", bias_fm[n][:], pb[2][:, 0:8], [pbuf[2]],
              [bias_fm_b[n]])

    x_tm_b = [Buf(f"x_tm{i}") for i in range(2)]
    x_fm_b = [Buf(f"x_fm{i}") for i in range(4)]

    def x_pre_block(j, half):
        fs = j
        bank = pb[half]
        for k4 in range(4):
            kc = half * 4 + k4
            transp(x_tm[:, j * D + kc * 128: j * D + (kc + 1) * 128],
                   bank[:, k4 * 128:(k4 + 1) * 128], ident[:],
                   [x_tm_b[j], ident_b], [pbuf[half]])
        split_evac(bank[:], pbuf[half],
                   x_hi[:, fs * D + half * 512: fs * D + (half + 1) * 512],
                   x_lo[:, fs * D + half * 512: fs * D + (half + 1) * 512],
                   [x_fm_b[fs], wtmp_b[0], bias_tm_b["bq"]])

    quantB = make_quantize(("wB", None), use_act=True, alias=(wfp, wfp_b))
    wfst_hi = [sb16(f"wfst_hi{i}", [128, 512], ws_scope) for i in range(3)]
    wfst_lo = [sb16(f"wfst_lo{i}", [128, 512], ws_scope) for i in range(3)]
    wfst_b = [Buf(f"wfst{i}") for i in range(3)]
    for sl in range(2):
        dma_load("sp", wraw[sl][:], w_in["wf"][sl * 128:(sl + 1) * 128, :],
                 [], [wraw_b[sl]])
    for jp in range(4):
        stA, stB = (2 * jp) % 3, (2 * jp + 1) % 3
        if jp < 3:
            nA = 2 * jp + 2
            dma_load("sp", wraw[nA % 3][:],
                     w_in["wf"][nA * 128:(nA + 1) * 128, :],
                     [], [wraw_b[nA % 3]])
        sA = quantA.steps(wraw[stA][:], wraw_b[stA], wraw[stA][:],
                          [wraw_b[stA]])
        sB = quantB.steps(wraw[stB][:], wraw_b[stB], wraw[stB][:],
                          [wraw_b[stB]])
        qsteps = [f for ab in zip(sA, sB) for f in ab]
        vblocks = []
        if jp < 3:
            for vsl in range(2):
                vjt = 2 * jp + 2 + vsl
                for vhalf in range(2):
                    vblocks.append(
                        lambda vsl=vsl, vjt=vjt, vhalf=vhalf: wv_block(
                            wvraw[vsl][:], wvraw_b[vsl], vjt, vhalf, vhalf))
        else:
            for j in range(2):
                dma_load("sp", x_tm[:, j * D:(j + 1) * D],
                         xp[j * 128:(j + 1) * 128, :],
                         [], [x_tm_b[j], (cosr_b, sinr_b)[j]])
            for j in range(2):
                for half in range(2):
                    vblocks.append(
                        lambda j=j, half=half: x_pre_block(j, half))
        ins_at = (5, 11, 17, 23)
        for i, f in enumerate(qsteps):
            f()
            if i in ins_at and vblocks:
                vblocks[ins_at.index(i)]()
        if jp < 2:
            for sl in range(2):
                vjt = 2 * jp + 4 + sl
                dma_load("act", wvraw[sl][:],
                         w_in["wv"][vjt * 128:(vjt + 1) * 128, :],
                         [], [wvraw_b[sl]])
        for sl in range(2):
            jt = 2 * jp + sl
            st = jt % 3
            for half in range(2):
                bank = pb[2 + half]
                wi = (jt * 2 + half) % 3
                for k4 in range(4):
                    kc = half * 4 + k4
                    transp(wraw[st][:, kc * 128:(kc + 1) * 128],
                           bank[:, k4 * 128:(k4 + 1) * 128], ident[:],
                           [wraw_b[st], ident_b], [pbuf[2 + half]])
                split_evac(bank[:].rearrange("p (a b) -> p a b", b=128),
                           pbuf[2 + half],
                           wfst_hi[wi][:].rearrange("p (a b) -> p a b",
                                                    b=128),
                           wfst_lo[wi][:].rearrange("p (a b) -> p a b",
                                                    b=128),
                           [wfst_b[wi]])
                for scr, stg in ((wf_scr_hi, wfst_hi[wi]),
                                 (wf_scr_lo, wfst_lo[wi])):
                    dst = bass.AP(scr.tensor,
                                  scr.offset + (half * 4) * D + jt * 128,
                                  [[8 * D, 128], [D, 4], [1, 128]])
                    P.dma("gp", lambda e, d=dst, s=stg: e.dma_start(
                        out=d, in_=s[:].rearrange("p (a b) -> p a b",
                                                  b=128)),
                        reads=[wfst_b[wi]], writes=[wf_scr_b])
        if jp < 3:
            nB = 2 * jp + 3
            dma_load("sp", wraw[nB % 3][:],
                     w_in["wf"][nB * 128:(nB + 1) * 128, :],
                     [], [wraw_b[nB % 3]])

    P.barrier()
    for g in reversed(ws_scope):
        g.__exit__(None, None, None)

    XR, FR, KR, VR, QR = 2, 4, 6, 6, 4
    a_scope = []
    k_fm = sb("k_fm", [128, KR * D], a_scope)
    k_fm_b = [Buf(f"k_fm{i}") for i in range(KR)]
    v_tm = sb("v_tm", [128, VR * VW], a_scope)
    v_tm_b = [Buf(f"v_tm{i}") for i in range(VR)]
    q_fm = sb("q_fm", [128, QR * D], a_scope)
    q_fm_b = [Buf(f"q_fm{i}") for i in range(QR)]
    NPR = 3
    probs = sb("probs", [128, NPR * 512], a_scope)
    probs_b = [Buf(f"probs{i}") for i in range(NPR)]
    ctxF = sb("ctxF", [64, 512], a_scope); ctxF_b = Buf("ctxF")
    zb_sb = sb("zb_sb", [64, 512], a_scope);  zb_sb_b = Buf("zb_sb")
    cxh = sb16("cxh", [64, 2 * D], a_scope)
    cxl = sb16("cxl", [64, 2 * D], a_scope)
    cxs_b = Buf("cxs")
    zr = sb("zr", [1, 512], a_scope);      zr_b = Buf("zr")

    ctx_scr_b = [Buf(f"ctx_scr{t}") for t in range(NT)]

    P.op("dve", lambda e: e.memset(
        strided(v_tm[:], 64, [[VW, VR], [65, H]]), 1.0),
        reads=[], writes=list(v_tm_b))

    def proj2(w_hi, w_lo, fm_w_b, bias_n, src_hi_fn, src_lo_fn, src_bufs,
              dst_ap_fn, dst_bufs, banks=None):
        if banks is None:
            banks = ((pb[2][:, 0:512], pbuf[2]), (pb[3][:, 0:512], pbuf[3]))
        if len(banks) == 2:
            banks = banks + banks
        for half in range(2):
            for bi in range(2):
                jcs = (half * 4 + 2 * bi, half * 4 + 2 * bi + 1)
                bank, bank_b = banks[half * 2 + bi]
                first = True
                for kc in range(8):
                    for ji, jc in enumerate(jcs):
                        wsl = slice(kc * D + jc * 128, kc * D + (jc + 1) * 128)
                        po = bank[:, ji * 256:(ji + 1) * 256]
                        last = kc == 7 and ji == 1
                        mm(w_hi[:, wsl], src_hi_fn(kc), po, first, False,
                           [fm_w_b] + list(src_bufs), [bank_b])
                        mm(w_lo[:, wsl], src_hi_fn(kc), po, False, False,
                           [fm_w_b] + list(src_bufs), [bank_b])
                        mm(w_hi[:, wsl], src_lo_fn(kc), po, False, last,
                           [fm_w_b] + list(src_bufs), [bank_b])
                        first = False
            for bi in range(2):
                jcs = (half * 4 + 2 * bi, half * 4 + 2 * bi + 1)
                bank, bank_b = banks[half * 2 + bi]
                for ji, jc in enumerate(jcs):
                    act(dst_ap_fn(jc), bank[:, ji * 256:(ji + 1) * 256],
                        AF.Identity, bias_fm[bias_n][:, jc:jc + 1], 1.0,
                        [bank_b, bias_fm_b[bias_n]], dst_bufs)

    pending_tail = []

    def flush_tail():
        while pending_tail:
            pending_tail.pop(0)()

    def att_group(tq, g):
        t = tq - 2
        qs = tq % QR
        cbank = pb[g % 2]
        cbank_b = pbuf[g % 2]

        def ctx_mms(kp):
            ky = tq - 2 + kp
            pslot = kp % NPR
            vs = ky % VR
            for h_loc in range(4):
                h = 4 * g + h_loc
                mm(v_tm[:, vs * VW + h * 65: vs * VW + h * 65 + 65],
                   probs[:, pslot * 512 + h_loc * 128:
                         pslot * 512 + (h_loc + 1) * 128],
                   cbank[0:65, h_loc * 128:(h_loc + 1) * 128],
                   kp == 0 and h_loc == 0, kp == 4 and h_loc == 3,
                   [v_tm_b[vs], probs_b[pslot]], [cbank_b])

        if True:
            for kp in range(5):
                ky = tq - 2 + kp
                sbank = psE if kp % 2 == 0 else psO
                sbank_b = psE_b if kp % 2 == 0 else psO_b
                ks = ky % KR
                for h_loc in range(4):
                    ch = 2 * g + h_loc // 2
                    hh = h_loc % 2
                    r0 = hh * 64
                    mm(k_fm[r0:r0 + 64,
                            ks * D + ch * 128: ks * D + (ch + 1) * 128],
                       q_fm[r0:r0 + 64,
                            qs * D + ch * 128: qs * D + (ch + 1) * 128],
                       sbank[:, hh * 512 + (h_loc // 2) * 128:
                             hh * 512 + (h_loc // 2) * 128 + 128],
                       True, True,
                       [k_fm_b[ks], q_fm_b[qs]], [sbank_b],
                       tp=(r0, 0))
                pslot = kp % NPR
                p_ap = probs[:, pslot * 512:(pslot + 1) * 512]
                act(p_ap, strided(sbank[:], 0, [[128, 2], [512, 2], [1, 128]]),
                    AF.Exp, killb[:, ky:ky + 1], 1.0,
                    [sbank_b, killb_b], [probs_b[pslot]])
                if kp == 0 or kp == 4:
                    mbuf, mb = (m0, m0_b) if kp == 0 else (m4, m4_b)
                    msk = strided(mbuf[:], 0, [[0, 4], [1, 128]])
                    tt("dve", p_ap.rearrange("p (a b) -> p a b", b=128),
                       p_ap.rearrange("p (a b) -> p a b", b=128),
                       msk, ALU.mult, [probs_b[pslot], mb], [probs_b[pslot]])
                if kp >= 1:
                    ctx_mms(kp - 1)
            ctx_mms(4)
            if pending_tail:
                pending_tail.pop(0)()
            if pending_tail:
                pending_tail.pop(0)()
            act(zr[:], cbank[64:65, :], AF.Ln, 0.0, 1.0,
                [cbank_b], [zr_b])
            act(zr[:], zr[:], AF.Exp, 0.0, -1.0, [zr_b], [zr_b])

            def tail(g=g, cbank=cbank, cbank_b=cbank_b):
                mm(m4[0:1, 0:64], zr[:], pb[3][0:64, :], True, True,
                   [m4_b, zr_b], [pbuf[3]])
                vcopy("dve", zb_sb[:], pb[3][0:64, :], [pbuf[3]], [zb_sb_b])
                tt("dve", ctxF[:], cbank[0:64, :], zb_sb[:], ALU.mult,
                   [cbank_b, zb_sb_b], [ctxF_b])
            pending_tail.append(tail)

            def tail_split(g=g):
                gsl = slice(g * 512, (g + 1) * 512)
                vcopy("dve", cxh[:, gsl], ctxF[:], [ctxF_b], [cxs_b])
                tt("dve", cxl[:, gsl], ctxF[:], cxh[:, gsl],
                   ALU.subtract, [ctxF_b, cxs_b], [cxs_b])
            pending_tail.append(tail_split)

        if g == 3:
            def spill(t=t):
                for scr, cbuf in ((ctx_scr_hi, cxh), (ctx_scr_lo, cxl)):
                    dst = bass.AP(scr.tensor, scr.offset + t * 128 * D,
                                  [[D, 64], [64 * D, 2], [1, D]])
                    P.dma("gp", lambda e, d=dst, s=cbuf: e.dma_start(
                        out=d, in_=s[:].rearrange("p (a b) -> p a b", b=D)),
                        reads=[cxs_b], writes=[ctx_scr_b[t]])
            pending_tail.append(spill)

    def load_and_transpose(j):
        xs = j % XR
        fs = j % FR
        if j >= 2:
            dma_load("sp", x_tm[:, xs * D:(xs + 1) * D],
                     xp[j * 128:(j + 1) * 128, :], [], [x_tm_b[xs]])
        for half in range(2):
            bank = pb[2 + half]
            for k4 in range(4):
                kc = half * 4 + k4
                transp(x_tm[:, xs * D + kc * 128: xs * D + (kc + 1) * 128],
                       bank[:, k4 * 128:(k4 + 1) * 128], ident[:],
                       [x_tm_b[xs], ident_b], [pbuf[2 + half]])
            split_evac(bank[:], pbuf[2 + half],
                       x_hi[:, fs * D + half * 512: fs * D + (half + 1) * 512],
                       x_lo[:, fs * D + half * 512: fs * D + (half + 1) * 512],
                       [x_fm_b[fs]])

    for p in range(10):
        a = 2 * p
        j = a + 1
        flush_tail()
        chunks = []
        if p < 9:
            chunks.append(lambda a=a: load_and_transpose(a + 2))
            chunks.append(lambda a=a: load_and_transpose(a + 3))
        fsA = a % FR
        ksA = a % KR
        src_hi_fn = lambda kc, fsA=fsA: strided(
            x_hi[:], fsA * D + kc * 128, [[D, 2], [1, 128]])
        src_lo_fn = lambda kc, fsA=fsA: strided(
            x_lo[:], fsA * D + kc * 128, [[D, 2], [1, 128]])
        src_bufs = [x_fm_b[fsA], x_fm_b[(fsA + 1) % FR]]
        proj2(wk_hi, wk_lo, wk_fm_b, "bk", src_hi_fn, src_lo_fn,
              src_bufs,
              lambda jc, ksA=ksA: strided(
                  k_fm[:], ksA * D + jc * 128, [[D, 2], [1, 128]]),
              [k_fm_b[ksA], k_fm_b[(ksA + 1) % KR]])
        qsA = a % QR
        vscr_off = qsA * D
        vscr_bufs = [q_fm_b[qsA], q_fm_b[(qsA + 1) % QR]]
        proj2(wv_hi, wv_lo, wv_fm_b, "bv", src_hi_fn, src_lo_fn,
              src_bufs,
              lambda jc, o=vscr_off: strided(q_fm[:], o + jc * 128,
                                             [[D, 2], [1, 128]]),
              vscr_bufs)
        for ti in range(2):
            vs = (a + ti) % VR
            for half in range(2):
                bank = pb[half]
                for c4 in range(4):
                    ch = half * 4 + c4
                    transp(q_fm[:, vscr_off + ti * D + ch * 128:
                                 vscr_off + ti * D + (ch + 1) * 128],
                           bank[:, c4 * 128:(c4 + 1) * 128], ident[:],
                           [vscr_bufs[ti], ident_b], [pbuf[half]])
                dst = strided(v_tm[:], vs * VW + half * 4 * 130,
                              [[130, 4], [65, 2], [1, 64]])
                vcopy("dve", dst,
                      bank[:].rearrange("p (a c b) -> p a c b", a=4, b=64),
                      [pbuf[half]], [v_tm_b[vs]])
        if 3 <= j <= 17:
            def projq(a=a, qsA=qsA, src_hi_fn=src_hi_fn,
                      src_lo_fn=src_lo_fn, src_bufs=src_bufs):
                proj2(wq_hi, wq_lo, wq_fm_b, "bq", src_hi_fn, src_lo_fn,
                      src_bufs,
                      lambda jc, qsA=qsA: strided(
                          q_fm[:], qsA * D + jc * 128, [[D, 2], [1, 128]]),
                      [q_fm_b[qsA], q_fm_b[(qsA + 1) % QR]])
            chunks.append(projq)
        units = [(tq, g) for tq in (a - 2, a - 1) if 2 <= tq <= 17
                 for g in range(4)]
        if not units:
            for ch in chunks:
                ch()
        else:
            ci = 0
            for ui, (tq, g) in enumerate(units):
                att_group(tq, g)
                if ui % 2 == 1 and ci < len(chunks):
                    chunks[ci]()
                    ci += 1
            while ci < len(chunks):
                chunks[ci]()
                ci += 1

    flush_tail()

    P.barrier()
    for g in reversed(a_scope):
        g.__exit__(None, None, None)
    for g in reversed(xa_scope):
        g.__exit__(None, None, None)
    for g in reversed(w_scope):
        g.__exit__(None, None, None)

    b_scope = []
    wo_hi = sb16("wo_hi", [128, 8 * D], b_scope)
    wo_lo = sb16("wo_lo", [128, 8 * D], b_scope)
    wf_hi = sb16("wf_hi", [128, 8 * D], b_scope)
    wf_lo = sb16("wf_lo", [128, 8 * D], b_scope)
    wo_fm_b = Buf("wo_fm");  wf_fm_b = Buf("wf_fm")
    braw_b = [Buf(f"braw{i}") for i in range(2)]



    quantize = make_quantize(("b", b_scope))

    NBS = 3
    ctx_hi = sb16("ctx_hi", [128, NBS * 2 * D], b_scope)
    ctx_lo = sb16("ctx_lo", [128, NBS * 2 * D], b_scope)
    ctxs_b = [Buf(f"ctxs{i}") for i in range(NBS)]
    xres = sb("xres", [128, 2 * 2 * D], b_scope)
    xres_b = [Buf(f"xres{i}") for i in range(2)]
    afm = sb("afm", [128, NBS * 2 * D], b_scope)
    afm_b = [Buf(f"afm{i}") for i in range(NBS)]
    z1 = sb("z1", [128, 4 * D], b_scope)
    z1_b = [Buf(f"z1_{i}") for i in range(4)]
    xdt = sb("xdt", [128, 2 * D], b_scope)
    xdt_b = [Buf(f"xdt{i}") for i in range(2)]
    xdf_hi = sb16("xdf_hi", [128, 2 * D], b_scope)
    xdf_lo = sb16("xdf_lo", [128, 2 * D], b_scope)
    xdf_b = Buf("xdf")
    ffm_b = Buf("ffm")
    outs = sb("outs", [128, 2 * D], b_scope)
    outs_b = [Buf(f"outs{i}") for i in range(2)]
    ln_sum = sb("ln_sum", [128, 4], b_scope)
    ln_sq = sb("ln_sq", [128, 4], b_scope)
    ln_m = sb("ln_m", [128, 4], b_scope)
    ln_v = sb("ln_v", [128, 4], b_scope)
    ln_r = sb("ln_r", [128, 4], b_scope)
    lnsc = sb("lnsc", [128, 2 * D], b_scope)
    _lnsc_shared = [Buf("lnsc0"), Buf("lnsc1")]
    ln_bufs = [{**{n: Buf(n + str(i)) for n in
                   ("ln_sum", "ln_sq", "ln_m", "ln_v", "ln_r")},
                "lnsc": _lnsc_shared[i % 2]}
               for i in range(4)]
    out_hbm_b = [Buf(f"out{t}") for t in range(NT)]

    weight_to_fm(w_in["wo"], [xdt[:, 0:D], xdt[:, D:2 * D]], braw_b,
                 wo_hi, wo_fm_b, lo_t=wo_lo, queue="gp")
    dma_load("act", wf_hi[:], wf_scr_hi, [wf_scr_b], [wf_fm_b])
    dma_load("act", wf_lo[:], wf_scr_lo, [wf_scr_b], [wf_fm_b])

    def layernorm_steps(src_ap, src_b, dst_ap, dst_b, sl=0):
        B = ln_bufs[sl]
        l_sum = ln_sum[:, sl:sl + 1]
        l_sq = ln_sq[:, sl:sl + 1]
        l_m = ln_m[:, sl:sl + 1]
        l_v = ln_v[:, sl:sl + 1]
        l_r = ln_r[:, sl:sl + 1]
        l_sc = lnsc[:, (sl % 2) * D:(sl % 2 + 1) * D]
        st = []
        st.append(lambda: act(l_sc, src_ap, AF.Identity, 0.0, 1.0, [src_b],
                              [B["lnsc"], B["ln_sum"]], accum=l_sum))
        st.append(lambda: act(l_sc, src_ap, AF.Square, 0.0, 1.0, [src_b],
                              [B["lnsc"], B["ln_sq"]], accum=l_sq))
        st.append(lambda: ts1("dve", l_m, l_sum, 1.0 / D, ALU.mult,
                              [B["ln_sum"]], [B["ln_m"]]))
        st.append(lambda: tt("dve", l_v, l_m, l_m, ALU.mult,
                             [B["ln_m"]], [B["ln_v"]]))
        st.append(lambda: P.op(
            "dve", lambda e, o=l_v, i=l_sq, m=l_v:
            e.scalar_tensor_tensor(o, i, 1.0 / D, m, ALU.mult, ALU.subtract),
            reads=[B["ln_sq"], B["ln_v"]], writes=[B["ln_v"]]))
        st.append(lambda: act(l_v, l_v, AF.Sqrt, ceps[:], 1.0,
                              [B["ln_v"], ceps_b], [B["ln_v"]]))
        st.append(lambda: vrecip(l_r, l_v, [B["ln_v"]], [B["ln_r"]]))
        rbc = bass.AP(l_r.tensor, l_r.offset,
                      [list(l_r.ap[0]), [0, src_ap.free_size()]])
        st.append(lambda: P.op(
            "dve", lambda e, o=dst_ap, i=src_ap, m=l_m, r=rbc:
            e.scalar_tensor_tensor(o, i, m, r, ALU.subtract, ALU.mult),
            reads=[src_b, B["ln_m"], B["ln_r"]], writes=list(dst_b)))
        return st

    def layernorm(src_ap, src_b, dst_ap, dst_b, sl=0):
        for f in layernorm_steps(src_ap, src_b, dst_ap, dst_b, sl=sl):
            f()

    def b_ctx_loads(gi):
        cs = gi % NBS
        coff = cs * 2 * D
        for ti in range(2):
            t = 2 * gi + ti
            for scr, dstt in ((ctx_scr_hi, ctx_hi), (ctx_scr_lo, ctx_lo)):
                dma_load("sp", dstt[:, coff + ti * D: coff + (ti + 1) * D],
                         scr[t * 128:(t + 1) * 128, :],
                         [ctx_scr_b[t]], [ctxs_b[cs]])

    def b_xres_loads(gi):
        gs = gi % 2
        goff = gs * 2 * D
        for ti in range(2):
            t = 2 * gi + ti
            dma_load("act", xres[:, goff + ti * D: goff + (ti + 1) * D],
                     xp[(t + 2) * 128:(t + 3) * 128, :], [], [xres_b[gs]])

    def b_oproj(gi):
        cs = gi % NBS
        coff = cs * 2 * D
        proj2(wo_hi, wo_lo, wo_fm_b, "bo",
              lambda kc, coff=coff: strided(ctx_hi[:], coff + kc * 128,
                                            [[D, 2], [1, 128]]),
              lambda kc, coff=coff: strided(ctx_lo[:], coff + kc * 128,
                                            [[D, 2], [1, 128]]),
              [ctxs_b[cs]],
              lambda jc, cs=cs: strided(afm[:], cs * 2 * D + jc * 128,
                                        [[D, 2], [1, 128]]),
              [afm_b[cs]],
              banks=((pb[2][:, 0:512], pbuf[2]), (pb[3][:, 0:512], pbuf[3]),
                     (psE[:, 0:512], psE_b), (psO[:, 0:512], psO_b)))

    def b_front(gi):
        gs = gi % 2
        cs = gi % NBS
        goff = gs * 2 * D
        for ti in range(2):
            aoff = cs * 2 * D + ti * D
            for half in range(2):
                bank = pb[half]
                for c4 in range(4):
                    jc = half * 4 + c4
                    transp(afm[:, aoff + jc * 128: aoff + (jc + 1) * 128],
                           bank[:, c4 * 128:(c4 + 1) * 128], ident[:],
                           [afm_b[cs], ident_b], [pbuf[half]])
                sl_x = xres[:, goff + ti * D + half * 512:
                            goff + ti * D + (half + 1) * 512]
                tt("dve", sl_x, bank[:], sl_x, ALU.add,
                   [pbuf[half], xres_b[gs]], [xres_b[gs]])
    def b_ln1_quant_steps(gi):
        gs = gi % 2
        goff = gs * 2 * D
        lns = [layernorm_steps(
            xres[:, goff + ti * D: goff + (ti + 1) * D], xres_b[gs],
            z1[:, goff + ti * D: goff + (ti + 1) * D],
            [z1_b[gs * 2 + ti]], sl=gs * 2 + ti) for ti in range(2)]
        st = [f for ab in zip(lns[0], lns[1]) for f in ab]
        for ti in range(2):
            st += quantize.steps(z1[:, goff + ti * D: goff + (ti + 1) * D],
                                 z1_b[gs * 2 + ti],
                                 xdt[:, ti * D:(ti + 1) * D], [xdt_b[ti]])
        return st

    def b_ffn_transp(ti):
        for half in range(2):
            bank = pb[half]
            for c4 in range(4):
                kc = half * 4 + c4
                transp(xdt[:, ti * D + kc * 128: ti * D + (kc + 1) * 128],
                       bank[:, c4 * 128:(c4 + 1) * 128], ident[:],
                       [xdt_b[ti], ident_b], [pbuf[half]])
            split_evac(bank[:], pbuf[half],
                       xdf_hi[:, ti * D + half * 512:
                              ti * D + (half + 1) * 512],
                       xdf_lo[:, ti * D + half * 512:
                              ti * D + (half + 1) * 512],
                       [xdf_b])

    def b_ffn(gi):
        for ti in range(2):
            b_ffn_transp(ti)
        ffm_off = (gi % NBS) * 2 * D
        proj2(wf_hi, wf_lo, wf_fm_b, "bf",
              lambda kc: strided(xdf_hi[:], kc * 128, [[D, 2], [1, 128]]),
              lambda kc: strided(xdf_lo[:], kc * 128, [[D, 2], [1, 128]]),
              [xdf_b],
              lambda jc, o=ffm_off: strided(afm[:], o + jc * 128,
                                            [[D, 2], [1, 128]]),
              [ffm_b],
              banks=((psE[:, 512:1024], psE_b), (psO[:, 512:1024], psO_b),
                     (pb[2][:, 0:512], pbuf[2]), (pb[3][:, 0:512], pbuf[3])))

    def b_tail_steps(gi):
        gs = gi % 2
        goff = gs * 2 * D
        ffm_off = (gi % NBS) * 2 * D
        st = []
        for ti in range(2):
            t = 2 * gi + ti
            for half in range(2):
                def blk(ti=ti, half=half):
                    bank = pb[half]
                    for c4 in range(4):
                        jc = half * 4 + c4
                        transp(afm[:, ffm_off + ti * D + jc * 128:
                                     ffm_off + ti * D + (jc + 1) * 128],
                               bank[:, c4 * 128:(c4 + 1) * 128], ident[:],
                               [ffm_b, ident_b], [pbuf[half]])
                    sl_z = z1[:, goff + ti * D + half * 512:
                              goff + ti * D + (half + 1) * 512]
                    tt("dve", sl_z, bank[:], sl_z, ALU.add,
                       [pbuf[half], z1_b[gs * 2 + ti]],
                       [z1_b[gs * 2 + ti]])
                st.append(blk)
            st += layernorm_steps(z1[:, goff + ti * D: goff + (ti + 1) * D],
                                  z1_b[gs * 2 + ti],
                                  outs[:, ti * D:(ti + 1) * D],
                                  [outs_b[ti]], sl=gs * 2 + ti)
            st.append(lambda t=t, ti=ti: P.dma(
                "gp", lambda e, t=t, ti=ti: e.dma_start(
                    out=out_hbm[t * 128:(t + 1) * 128, :],
                    in_=outs[:, ti * D:(ti + 1) * D]),
                reads=[outs_b[ti]], writes=[out_hbm_b[t]]))
        return st

    b_ctx_loads(0)
    b_xres_loads(0)
    b_oproj(0)
    b_ctx_loads(1)
    b_oproj(1)
    for gi in range(NT // 2):
        b_front(gi)
        sA = b_ln1_quant_steps(gi)
        sB = b_tail_steps(gi - 1) if gi > 0 else []
        for x in range(max(len(sA), len(sB))):
            if x < len(sA):
                sA[x]()
            if x < len(sB):
                sB[x]()
        if gi + 1 < NT // 2:
            b_xres_loads(gi + 1)
        if gi + 2 < NT // 2:
            b_ctx_loads(gi + 2)
            b_oproj(gi + 2)
        b_ffn(gi)
    for f in b_tail_steps(NT // 2 - 1):
        f()

    P.op("gp", None, reads=list(out_hbm_b), writes=[])

    counts = P.finalize()
    sem_keys = list(ENGINES) + [("dma", e, r) for e in ("sp", "gp", "act")
                                for r in range(NROT)]
    sems = {}
    sem_guards = []
    for k in sem_keys:
        nm = k if isinstance(k, str) else f"dma_{k[1]}_{k[2]}"
        g = nc.semaphore(f"sem_{nm}")
        sems[k] = g.__enter__()
        sem_guards.append(g)
    blk = nc.Block()
    block = blk.__enter__()

    @block.tensor
    def _(e):
        P.emit(e, "pe", sems)

    @block.scalar
    def _(e):
        P.emit(e, "act", sems)

    @block.vector
    def _(e):
        P.emit(e, "dve", sems)

    @block.gpsimd
    def _(e):
        P.emit(e, "gp", sems)

    @block.sync
    def _(e):
        P.emit(e, "sp", sems)

    blk.__exit__(None, None, None)
    for g in reversed(sem_guards):
        g.__exit__(None, None, None)
    for g in reversed(b_scope):
        g.__exit__(None, None, None)
    for g in reversed(guards):
        g.__exit__(None, None, None)
    return nc


def make_inputs(L0_attn_x, L0_attn_Wq, L0_attn_bq, L0_attn_Wk, L0_attn_bk,
                L0_attn_Wv, L0_attn_bv, L0_attn_Wo, L0_attn_bo,
                L0_ffn_weight, L0_ffn_bias):
    x = np.asarray(L0_attn_x, dtype=np.float32)
    B, S, _ = x.shape
    per = S // 4
    in_maps = []
    shared = {
        "wq": np.ascontiguousarray(L0_attn_Wq, dtype=np.float32),
        "wk": np.ascontiguousarray(L0_attn_Wk, dtype=np.float32),
        "wv": np.ascontiguousarray(L0_attn_Wv, dtype=np.float32),
        "wo": np.ascontiguousarray(
            np.asarray(L0_attn_Wo, dtype=np.float32)[:, _WO_PERM]),
        "wf": np.ascontiguousarray(L0_ffn_weight, dtype=np.float32),
        "bq": np.ascontiguousarray(np.asarray(L0_attn_bq, dtype=np.float32)
                                   .reshape(1, D)),
        "bk": np.ascontiguousarray(np.asarray(L0_attn_bk, dtype=np.float32)
                                   .reshape(1, D)),
        "bv": np.ascontiguousarray(np.asarray(L0_attn_bv, dtype=np.float32)
                                   .reshape(1, D)),
        "bo": np.ascontiguousarray(np.asarray(L0_attn_bo, dtype=np.float32)
                                   .reshape(1, D)),
        "bf": np.ascontiguousarray(np.asarray(L0_ffn_bias, dtype=np.float32)
                                   .reshape(1, D)),
    }
    for c in range(8):
        b, s = divmod(c, 4)
        lo, hi = s * per, (s + 1) * per
        pad_l = min(256, lo)
        pad_r = min(256, S - hi)
        sl = x[b, lo - pad_l: hi + pad_r]
        xp_arr = np.zeros((NP * T, D), dtype=np.float32)
        xp_arr[256 - pad_l: 256 - pad_l + sl.shape[0]] = sl
        kill = np.full(NP * T, -MHAT, dtype=np.float32)
        if pad_l < 256:
            kill[:256 - pad_l] += KILL
        if pad_r < 256:
            kill[NP * T - (256 - pad_r):] += KILL
        killb = np.ascontiguousarray(
            kill.reshape(NP, T).T)
        m = dict(shared)
        m["xp"] = xp_arr
        m["killb_hbm"] = killb
        in_maps.append(m)
    return in_maps


_NC_CACHE = None


def kernel(L0_attn_x, L0_attn_Wq, L0_attn_bq, L0_attn_Wk, L0_attn_bk,
           L0_attn_Wv, L0_attn_bv, L0_attn_Wo, L0_attn_bo,
           L0_ffn_weight, L0_ffn_bias):
    global _NC_CACHE
    if _NC_CACHE is None:
        _NC_CACHE = build_nc()
    nc = _NC_CACHE
    in_maps = make_inputs(L0_attn_x, L0_attn_Wq, L0_attn_bq, L0_attn_Wk,
                          L0_attn_bk, L0_attn_Wv, L0_attn_bv, L0_attn_Wo,
                          L0_attn_bo, L0_ffn_weight, L0_ffn_bias)
    res = run_bass_kernel_spmd(nc, in_maps, core_ids=list(range(8)))
    x = np.asarray(L0_attn_x)
    B, S, _ = x.shape
    per = S // 4
    out = np.empty((B, S, D), dtype=np.float32)
    for c in range(8):
        b, s = divmod(c, 4)
        out[b, s * per:(s + 1) * per] = res.results[c]["out"]
    return out
